# version 10
# baseline (speedup 1.0000x reference)
import math

import numpy as np
import ml_dtypes
import concourse.bass as bass
import concourse.mybir as mybir
from concourse.bass_utils import run_bass_kernel_spmd

F32 = mybir.dt.float32
BF16 = mybir.dt.bfloat16
I32 = mybir.dt.int32
AF = mybir.ActivationFunctionType
ALU = mybir.AluOpType

D = 1024
H = 16
HD = 64
NCH = 8
T = 2048
GSZ = 512
NG = T // GSZ
CH = 128
CPG = GSZ // CH
FFN_H = 3072
HC = FFN_H // 128
TH = T // 2
EPS = 1e-5
QS = 1.0 / 127.0

PI = math.pi
MAGIC = 12582912.0
INV2PI = float(np.float32(1.0 / (2 * PI)))
NTWO_PI = float(np.float32(-(2 * PI)))
f32i = np.float32

ENGS = ["sync", "gpsimd", "tensor", "vector", "scalar"]
import os as _os
K_LEVEL = int(_os.environ.get("K_LEVEL", "7"))
K_SUB = int(_os.environ.get("K_SUB", "5"))
K_DUMP = _os.environ.get("K_DUMP", "")


def build_program(detect_races=False):
    nc = bass.Bass("TRN2", target_bir_lowering=False, detect_race_conditions=detect_races)

    xT = nc.dram_tensor("xT", [D, T], F32, kind="ExternalInput")
    xTp = nc.dram_tensor("xTp", [D, T], F32, kind="ExternalInput")
    wq_d = nc.dram_tensor("wq", [D, D], F32, kind="ExternalInput")
    wk_d = nc.dram_tensor("wk", [D, D], F32, kind="ExternalInput")
    wv_d = nc.dram_tensor("wv", [D, D], F32, kind="ExternalInput")
    wo_d = nc.dram_tensor("wo", [D, D], F32, kind="ExternalInput")
    g1_d = nc.dram_tensor("g1", [D, HD], F32, kind="ExternalInput")
    g2_d = nc.dram_tensor("g2", [HD, D], F32, kind="ExternalInput")
    gamma_d = nc.dram_tensor("gamma", [128, NCH], F32, kind="ExternalInput")
    beta_d = nc.dram_tensor("beta", [128, NCH], F32, kind="ExternalInput")
    theta_d = nc.dram_tensor("theta", [128, NCH], F32, kind="ExternalInput")
    cbrow_d = nc.dram_tensor("cbrow", [128, GSZ], F32, kind="ExternalInput")
    threp_d = nc.dram_tensor("threp", [128, GSZ], F32, kind="ExternalInput")
    wgT_d = nc.dram_tensor("wgT", [D, FFN_H], I32, kind="ExternalInput")
    wuT_d = nc.dram_tensor("wuT", [D, FFN_H], I32, kind="ExternalInput")
    wdT_d = nc.dram_tensor("wdT", [FFN_H, D], I32, kind="ExternalInput")
    outT = nc.dram_tensor("outT", [D, T], F32, kind="ExternalOutput")

    ident_d = nc.inline_tensor(np.eye(128, dtype=ml_dtypes.bfloat16), "c_ident")
    ones2_d = nc.inline_tensor(np.ones((128, 2), dtype=ml_dtypes.bfloat16), "c_ones2")
    half2_d = nc.inline_tensor(np.full((2, 128), 0.5, dtype=np.float32), "c_half2")
    tri = np.triu(np.ones((CH, CH), dtype=np.float32))
    mask_d = nc.inline_tensor(np.concatenate([tri] * 4, axis=1).astype(ml_dtypes.bfloat16), "c_mask")
    posu_d = nc.inline_tensor(
        np.broadcast_to(np.arange(GSZ, dtype=np.float32), (2, GSZ)).copy(), "c_posu")
    magic_d = nc.inline_tensor(np.full((128, 1), 12582912.0, dtype=np.float32), "c_magic")

    cur = [16416]

    def place(name, shape, dtype, at=None):
        nbytes = shape[1] * (2 if dtype in (BF16,) else 4)
        if at is None:
            at = cur[0]
            cur[0] += (nbytes + 31) // 32 * 32
            assert cur[0] <= 229344, f"sbuf overflow at {name}: {cur[0]}"
        return nc.alloc_sbuf_tensor_at(name, shape, dtype, offset=at)

    s_ident = place("s_ident", [128, 128], BF16)
    s_ones2 = place("s_ones2", [128, 2], BF16)
    s_half2 = place("s_half2", [2, 128], F32)
    s_mask = place("s_mask", [128, 4 * CH], BF16)
    s_gamma = place("s_gamma", [128, NCH], F32)
    s_beta = place("s_beta", [128, NCH], F32)
    s_theta = place("s_theta", [128, NCH], F32)
    s_thsc = place("s_thsc", [128, NCH], F32)
    s_magic = place("s_magic", [128, 1], F32)
    s_coscb = place("s_coscb", [128, GSZ], F32)
    s_sincb = place("s_sincb", [128, GSZ], F32)
    tab0 = cur[0]
    s_cosu = place("s_cosu", [128, NCH * GSZ], BF16)
    s_sinu = place("s_sinu", [128, NCH * GSZ], BF16)
    misc0 = cur[0]
    s_tsc1 = place("s_tsc1", [128, GSZ], F32)
    s_tsc2 = place("s_tsc2", [128, GSZ], F32)
    s_tsc3 = place("s_tsc3", [128, GSZ], F32)
    s_tsc4 = place("s_tsc4", [128, GSZ], F32)
    s_posu2 = place("s_posu2", [2, GSZ], F32)
    s_rows = place("s_rows", [2, 3 * GSZ], F32)
    xg0 = cur[0]
    s_xg = place("s_xg", [128, NCH * GSZ], F32)
    s_xn = place("s_xn", [128, NCH * T], BF16)
    oat0 = cur[0]
    s_oat = place("s_oat", [128, NCH * T], BF16)
    qk0 = cur[0]
    s_qc = place("s_qc", [128, NCH * GSZ], BF16)
    s_qs = place("s_qs", [128, NCH * GSZ], BF16)
    s_kc = place("s_kc", [128, NCH * GSZ], BF16)
    s_ks = place("s_ks", [128, NCH * GSZ], BF16)
    w0 = cur[0]
    s_wq = place("s_wq", [128, NCH * D], BF16)
    s_wk = place("s_wk", [128, NCH * D], BF16)
    s_wv = place("s_wv", [128, NCH * D], BF16)
    small0 = cur[0]
    s_ktm = place("s_ktm", [128, 2 * D], BF16)
    s_qraw = place("s_qraw", [128, GSZ], BF16)
    s_uc = place("s_uc", [128, GSZ], BF16)
    s_us = place("s_us", [128, GSZ], BF16)
    s_em = place("s_em", [128, 4 * CH], BF16)
    s_scb = place("s_scb", [128, 512], BF16)
    s_ssb = place("s_ssb", [128, 512], BF16)

    top_used = cur[0]
    assert top_used <= 229344, f"over budget: {top_used}"

    s_stc = place("s_stc", [128, 512], F32, at=misc0 + 2 * GSZ * 4)
    s_sts = place("s_sts", [128, 512], F32, at=misc0 + 3 * GSZ * 4)
    s_xgb = place("s_xgb", [128, NCH * GSZ], BF16, at=qk0 + NCH * GSZ * 2)
    s_xg2 = place("s_xg2", [128, NCH * GSZ], BF16, at=qk0)
    s_xnp = place("s_xnp", [128, NCH * GSZ], BF16, at=oat0)
    s_vtm = place("s_vtm", [128, CPG * D], BF16, at=xg0)
    s_wo = place("s_wo", [128, NCH * D], BF16, at=w0)
    s_s = place("s_s", [128, NCH * T], BF16, at=w0 + NCH * D * 2)
    s_gate = place("s_gate", [128, NCH * T], BF16, at=qk0)
    s_spre = place("s_spre", [128, NCH * GSZ], F32, at=xg0)
    s_spb = place("s_spb", [128, NCH * GSZ], BF16, at=tab0)
    s_sp2 = place("s_sp2", [128, NCH * GSZ], BF16, at=tab0 + NCH * GSZ * 2)
    s_xng1 = place("s_xng1", [64, T], BF16, at=misc0)
    s_g1 = place("s_g1", [128, NCH * HD], BF16, at=misc0 + 2 * GSZ * 4)
    s_g2 = place("s_g2", [64, D], BF16, at=misc0 + 3 * GSZ * 4)
    s_h1 = place("s_h1", [128, (HC // 2) * TH], BF16, at=qk0)
    s_h2 = place("s_h2", [128, (HC // 2) * TH], BF16, at=oat0)
    s_wgu = place("s_wgu", [128, 4 * NCH * 128], BF16, at=tab0)
    s_wd = place("s_wd", [128, 2 * HC * 128], BF16, at=xg0)
    s_silu = place("s_silu", [128, 2 * GSZ], F32, at=misc0)
    s_fin = place("s_fin", [128, 2 * GSZ], F32, at=misc0 + 2 * GSZ * 4)
    s_xres = place("s_xres", [128, 2 * GSZ], F32, at=small0)

    P = nc.alloc_psum_tensor
    p_b0 = P("p_b0", [128, 512], F32)
    p_b1 = P("p_b1", [128, 512], F32)
    p_b2 = P("p_b2", [128, 512], F32)
    p_b3 = P("p_b3", [128, 512], F32)
    p_uc = P("p_uc", [128, 512], F32)
    p_us = P("p_us", [128, 512], F32)
    p_tr = P("p_tr", [128, 1024], BF16)
    p_b7 = P("p_b7", [128, 512], F32)

    with (
        nc.semaphore("sx") as sem_sx,
        nc.semaphore("sw") as sem_sw,
        nc.semaphore("pe") as sem_pe,
        nc.semaphore("dv") as sem_dv,
        nc.semaphore("ac") as sem_ac,
        nc.semaphore("so") as sem_so,
        nc.Block() as block,
    ):
        sems = {"sx": sem_sx, "sw": sem_sw, "pe": sem_pe,
                "dv": sem_dv, "ac": sem_ac, "so": sem_so}

        def program(eng, B):
            cnt = {k: 0 for k in sems}
            marks = {}
            state = {"started": False, "read_dv": 0, "write_pe": 0}

            def on(e):
                return eng == e

            def wait(e, sem, val=None):
                v = cnt[sem] if val is None else val
                if eng == e and v > 0:
                    B.wait_ge(sems[sem], v)

            def blk(sem, last_inst, n=1):
                cnt[sem] += n
                if last_inst is not None:
                    last_inst.then_inc(sems[sem], n)
                return cnt[sem]

            def mark(name):
                marks[name] = dict(cnt)

            def wait_mark(e, name, sem):
                m = marks.get(name)
                if m is not None:
                    wait(e, sem, m[sem])

            def fm_w(t):
                return t.ap().rearrange("(c p) n -> p c n", p=128)

            def sb3(ap_slice, n):
                return ap_slice.rearrange("p (c n) -> p c n", n=n)

            def fm_x3(t):
                return t.ap().rearrange("(c p) n -> p c n", p=128)

            def ld(dst, src):
                inst = B.dma_start(dst, src) if on("sync") else None
                blk("sx", inst, 16)

            ld(s_ident[:], ident_d.ap())
            ld(s_ones2[:], ones2_d.ap())
            ld(s_half2[:], half2_d.ap())
            ld(s_mask[:], mask_d.ap())
            ld(s_gamma[:], gamma_d.ap())
            ld(s_beta[:], beta_d.ap())
            ld(s_theta[:], theta_d.ap())
            ld(s_posu2[:], posu_d.ap())
            ld(s_magic[:], magic_d.ap())
            mark("p0")

            def wcast(dst, src):
                inst = B.dma_start(dst, src) if on("gpsimd") else None
                blk("sw", inst, 16)

            wcast(sb3(s_wk[:], D), fm_w(wk_d))
            wcast(sb3(s_wv[:], D), fm_w(wv_d))
            wcast(sb3(s_wq[:], D), fm_w(wq_d))
            mark("wkv")

            wait_mark("vector", "p0", "sx")
            i = B.tensor_scalar(s_thsc[:], s_theta[:], INV2PI, None, ALU.mult) if on("vector") else None
            blk("dv", i)

            NGB = 2 * NG
            inst = B.dma_start(s_tsc4[:], cbrow_d.ap()) if on("sync") else None
            blk("sx", inst, 16)
            inst = B.dma_start(s_tsc3[:], threp_d.ap()) if on("sync") else None
            blk("sx", inst, 16)
            wait("vector", "sx")
            i = None
            if on("vector"):
                B.tensor_tensor(s_tsc1[:], s_tsc4[:], s_tsc3[:], ALU.mult)
                B.tensor_scalar(s_tsc2[:], s_tsc1[:], INV2PI, None, ALU.mult)
                B.tensor_scalar(s_tsc2[:], s_tsc2[:], s_magic[:, 0:1], None, ALU.add)
                B.tensor_scalar(s_tsc2[:], s_tsc2[:], -MAGIC, None, ALU.add)
                B.scalar_tensor_tensor(s_sincb[:], s_tsc2[:], NTWO_PI, s_tsc1[:], ALU.mult, ALU.add)
                B.tensor_scalar(s_tsc3[:], s_tsc1[:], INV2PI, None, ALU.mult)
                B.tensor_scalar(s_tsc3[:], s_tsc3[:], 0.25, None, ALU.add)
                B.tensor_scalar(s_tsc3[:], s_tsc3[:], s_magic[:, 0:1], None, ALU.add)
                B.tensor_scalar(s_tsc3[:], s_tsc3[:], -MAGIC, None, ALU.add)
                B.scalar_tensor_tensor(s_coscb[:], s_tsc3[:], NTWO_PI, s_tsc1[:], ALU.mult, ALU.add)
                B.tensor_scalar(s_sincb[:], s_sincb[:], PI, None, ALU.min)
                B.tensor_scalar(s_sincb[:], s_sincb[:], -PI, None, ALU.max)
                B.tensor_scalar(s_coscb[:], s_coscb[:], PI / 2, None, ALU.min)
                i = B.tensor_scalar(s_coscb[:], s_coscb[:], -1.5 * PI, None, ALU.max)
            blk("dv", i)
            wait("scalar", "dv")
            i = B.activation(s_sincb[:], s_sincb[:], AF.Sin) if on("scalar") else None
            blk("ac", i)
            i = B.activation(s_coscb[:], s_coscb[:], AF.Sin, bias=f32i(PI / 2)) if on("scalar") else None
            blk("ac", i)

            wait("tensor", "dv")
            i = B.matmul(p_b7[:, 0:GSZ], s_half2[:], s_posu2[:], start=True, stop=True) if on("tensor") else None
            blk("pe", i)
            wait("vector", "pe")
            i = B.tensor_copy(s_tsc4[:], p_b7[:, 0:GSZ]) if on("vector") else None
            blk("dv", i)
            for c in range(NCH):
                col = c * GSZ
                wait("vector", "ac")
                i = None
                if on("vector"):
                    B.tensor_scalar(s_tsc1[:], s_tsc4[:], s_theta[:, c:c + 1], None, ALU.mult)
                    B.tensor_scalar(s_tsc2[:], s_tsc4[:], s_thsc[:, c:c + 1], None, ALU.mult)
                    B.tensor_scalar(s_tsc2[:], s_tsc2[:], s_magic[:, 0:1], None, ALU.add)
                    B.tensor_scalar(s_tsc2[:], s_tsc2[:], -MAGIC, None, ALU.add)
                    B.scalar_tensor_tensor(s_tsc2[:], s_tsc2[:], NTWO_PI, s_tsc1[:], ALU.mult, ALU.add)
                    B.tensor_scalar(s_tsc3[:], s_tsc4[:], s_thsc[:, c:c + 1], None, ALU.mult)
                    B.tensor_scalar(s_tsc3[:], s_tsc3[:], 0.25, None, ALU.add)
                    B.tensor_scalar(s_tsc3[:], s_tsc3[:], s_magic[:, 0:1], None, ALU.add)
                    B.tensor_scalar(s_tsc3[:], s_tsc3[:], -MAGIC, None, ALU.add)
                    B.scalar_tensor_tensor(s_tsc1[:], s_tsc3[:], NTWO_PI, s_tsc1[:], ALU.mult, ALU.add)
                    B.tensor_scalar(s_tsc2[:], s_tsc2[:], PI, None, ALU.min)
                    B.tensor_scalar(s_tsc2[:], s_tsc2[:], -PI, None, ALU.max)
                    B.tensor_scalar(s_tsc1[:], s_tsc1[:], PI / 2, None, ALU.min)
                    i = B.tensor_scalar(s_tsc1[:], s_tsc1[:], -1.5 * PI, None, ALU.max)
                blk("dv", i)
                wait("scalar", "dv")
                i = B.activation(s_sinu[:, col:col + GSZ], s_tsc2[:], AF.Sin) if on("scalar") else None
                blk("ac", i)
                i = B.activation(s_cosu[:, col:col + GSZ], s_tsc1[:], AF.Sin, bias=f32i(PI / 2)) if on("scalar") else None
                blk("ac", i)
            i = None
            if on("vector"):
                B.memset(s_stc[:], 0.0)
                i = B.memset(s_sts[:], 0.0)
            blk("dv", i)
            mark("uadd")
            mark("tables")

            def ln_group(xsrc, gcol, xn_dst, xn_col, xn_str):
                wait_mark("sync", "xg_free", "dv")
                wait("sync", "pe")
                inst = None
                if on("sync"):
                    inst = B.dma_start(
                        s_xg[:].rearrange("p (c n) -> p c n", n=GSZ),
                        fm_x3(xsrc)[:, :, gcol:gcol + GSZ])
                blk("sx", inst, 16)
                wait("gpsimd", "pe")
                inst = None
                if on("gpsimd"):
                    inst = B.dma_start(
                        s_xgb[:].rearrange("p (c n) -> p c n", n=GSZ),
                        fm_x3(xsrc)[:, :, gcol:gcol + GSZ])
                blk("sw", inst, 16)
                wait("scalar", "sx")
                wait("scalar", "pe")
                i2 = None
                if on("scalar"):
                    for c in range(NCH):
                        i2 = B.activation(s_xg2[:, c * GSZ:(c + 1) * GSZ], s_xg[:, c * GSZ:(c + 1) * GSZ], AF.Square)
                blk("ac", i2)
                wait("tensor", "sx")
                wait("tensor", "sw")
                wait("tensor", "ac")
                wait("tensor", "dv")
                wait_mark("tensor", "b7_free", "dv")
                wait_mark("tensor", "bc_free", "dv")
                i2 = None
                if on("tensor"):
                    for c in range(NCH):
                        B.matmul(p_b7[0:2, 0:GSZ], s_ones2[:], s_xgb[:, c * GSZ:(c + 1) * GSZ],
                                 start=(c == 0), stop=(c == NCH - 1))
                    for c in range(NCH):
                        i2 = B.matmul(p_b0[0:2, 0:GSZ], s_ones2[:], s_xg2[:, c * GSZ:(c + 1) * GSZ],
                                      start=(c == 0), stop=(c == NCH - 1))
                blk("pe", i2)
                mark("stats_done")
                wait("vector", "pe")
                i2 = None
                if on("vector"):
                    B.tensor_scalar(s_rows[:, 0:GSZ], p_b7[0:2, 0:GSZ], 1.0 / D, None, ALU.mult)
                    B.tensor_scalar(s_rows[:, GSZ:2 * GSZ], p_b0[0:2, 0:GSZ], 1.0 / D, None, ALU.mult)
                    B.tensor_tensor(s_rows[:, 2 * GSZ:3 * GSZ], s_rows[:, 0:GSZ], s_rows[:, 0:GSZ], ALU.mult)
                    i2 = B.tensor_sub(s_rows[:, GSZ:2 * GSZ], s_rows[:, GSZ:2 * GSZ], s_rows[:, 2 * GSZ:3 * GSZ])
                blk("dv", i2)
                mark("b7_free")
                wait("scalar", "dv")
                i2 = None
                if on("scalar"):
                    B.activation(s_rows[:, 2 * GSZ:3 * GSZ], s_rows[:, GSZ:2 * GSZ], AF.Ln, bias=f32i(EPS))
                    i2 = B.activation(s_rows[:, GSZ:2 * GSZ], s_rows[:, 2 * GSZ:3 * GSZ], AF.Exp, scale=f32i(-0.5))
                blk("ac", i2)
                wait("vector", "ac")
                i2 = B.tensor_mul(s_rows[:, 2 * GSZ:3 * GSZ], s_rows[:, 0:GSZ], s_rows[:, GSZ:2 * GSZ]) if on("vector") else None
                blk("dv", i2)
                wait("tensor", "dv")
                i2 = None
                if on("tensor"):
                    B.matmul(p_b0[:, 0:GSZ], s_half2[:], s_rows[:, GSZ:2 * GSZ], start=True, stop=True)
                    i2 = B.matmul(p_b1[:, 0:GSZ], s_half2[:], s_rows[:, 2 * GSZ:3 * GSZ], start=True, stop=True)
                blk("pe", i2)
                wait("vector", "pe")
                i2 = None
                if on("vector"):
                    for c in range(NCH):
                        B.tensor_tensor(s_xg[:, c * GSZ:(c + 1) * GSZ], s_xg[:, c * GSZ:(c + 1) * GSZ],
                                        p_b0[:, 0:GSZ], ALU.mult)
                        i2 = B.tensor_tensor(xn_dst[:, xn_col + c * xn_str: xn_col + c * xn_str + GSZ],
                                             s_xg[:, c * GSZ:(c + 1) * GSZ], p_b1[:, 0:GSZ], ALU.subtract)
                blk("dv", i2)
                mark("xg_free")
                mark("bc_free")

            def lrpe_project(w_sb, xn_src, xn_col, xn_str, dst_c, dst_s, grp_idx):
                for m in range(NCH):
                    pb = (p_b0, p_b1, p_b2, p_b3)[m % 4]
                    gate = f"pj{m % 4}"
                    wait_mark("tensor", gate, "dv")
                    i2 = None
                    if on("tensor"):
                        for c in range(NCH):
                            i2 = B.matmul(pb[:, 0:GSZ],
                                          w_sb[:, c * D + m * 128: c * D + m * 128 + 128],
                                          xn_src[:, xn_col + c * xn_str: xn_col + c * xn_str + GSZ],
                                          start=(c == 0), stop=(c == NCH - 1))
                    blk("pe", i2)
                    wait("scalar", "pe")
                    wait_mark("scalar", "qraw_free", "dv")
                    i2 = B.activation(s_qraw[:], pb[:, 0:GSZ], AF.Silu) if on("scalar") else None
                    blk("ac", i2)
                    wait("vector", "ac")
                    i2 = None
                    if on("vector"):
                        cbcol = m * 64 + grp_idx * 8
                        B.tensor_mul(s_uc[:], s_qraw[:], s_cosu[:, m * GSZ:(m + 1) * GSZ])
                        B.tensor_mul(s_us[:], s_qraw[:], s_sinu[:, m * GSZ:(m + 1) * GSZ])
                        B.tensor_scalar(s_tsc1[:], s_uc[:], s_coscb[:, cbcol:cbcol + 1], None, ALU.mult)
                        B.tensor_scalar(s_tsc2[:], s_us[:], s_sincb[:, cbcol:cbcol + 1], None, ALU.mult)
                        B.tensor_sub(dst_c[:, m * GSZ:(m + 1) * GSZ], s_tsc1[:], s_tsc2[:])
                        B.tensor_scalar(s_tsc1[:], s_us[:], s_coscb[:, cbcol:cbcol + 1], None, ALU.mult)
                        i2 = B.scalar_tensor_tensor(dst_s[:, m * GSZ:(m + 1) * GSZ], s_uc[:],
                                                    s_sincb[:, cbcol:cbcol + 1], s_tsc1[:], ALU.mult, ALU.add)
                    blk("dv", i2)
                    mark(gate)
                    mark("qraw_free")

            def v_project(xn_src, xn_col, xn_str):
                for j in range(CPG):
                    for half in range(2):
                        it = j * 2 + half
                        pb = (p_b0, p_b1, p_b2, p_b3)[it % 4]
                        gate = f"pj{it % 4}"
                        wait_mark("tensor", gate, "dv")
                        i2 = None
                        if on("tensor"):
                            for c in range(NCH):
                                i2 = B.matmul(
                                    pb[:, 0:GSZ],
                                    xn_src[:, xn_col + c * xn_str + j * CH: xn_col + c * xn_str + j * CH + CH],
                                    s_wv[:, c * D + half * GSZ: c * D + (half + 1) * GSZ],
                                    start=(c == 0), stop=(c == NCH - 1))
                        blk("pe", i2)
                        wait("vector", "pe")
                        i2 = None
                        if on("vector"):
                            i2 = B.tensor_copy(s_vtm[:, j * D + half * GSZ: j * D + (half + 1) * GSZ], pb[:, 0:GSZ])
                        blk("dv", i2)
                        mark(gate)

            def chunk_attn(g, j, own):
                tok0 = j * CH

                for half in range(2):
                    wait_mark("tensor", "tr_free", "dv")
                    i2 = None
                    if on("tensor"):
                        for t in range(4):
                            c = half * 4 + t
                            B.transpose(p_tr[:, t * 256:t * 256 + 128],
                                        s_kc[:, c * GSZ + tok0: c * GSZ + tok0 + CH], s_ident[:])
                            i2 = B.transpose(p_tr[:, t * 256 + 128:t * 256 + 256],
                                             s_ks[:, c * GSZ + tok0: c * GSZ + tok0 + CH], s_ident[:])
                    blk("pe", i2)
                    wait("vector", "pe")
                    i2 = None
                    if on("vector"):
                        i2 = B.tensor_copy(s_ktm[:, half * 1024:(half + 1) * 1024], p_tr[:])
                    blk("dv", i2)
                    mark("tr_free")

                if own and K_SUB >= 2:
                    i2 = None
                    if on("vector"):
                        B.tensor_copy(s_scb[:], s_stc[:])
                        i2 = B.tensor_copy(s_ssb[:], s_sts[:])
                    blk("dv", i2)
                    state["read_dv"] = cnt["dv"]

                    for q4 in range(4 if K_SUB >= 3 else 0):
                        pbE = p_b0 if (q4 % 2 == 0) else p_b1
                        pbO = p_b2 if (q4 % 2 == 0) else p_b3
                        egate = f"ee{q4 % 2}"
                        ogate = f"oe{q4 % 2}"
                        wait_mark("tensor", egate, "dv")
                        wait_mark("tensor", "bc_free", "dv")
                        for _pj in range(4):
                            wait_mark("tensor", f"pj{_pj}", "dv")
                        i2 = None
                        if on("tensor"):
                            for idx, hh in enumerate((0, 2, 1, 3)):
                                h = q4 * 4 + hh
                                c, b = h // 2, (h % 2) * 64
                                kcol = c * GSZ + tok0
                                if (idx == 0 and q4 > 0) or idx == 2:
                                    B.matmul(p_b7[0:64, 0:64], s_ident[:, 0:64],
                                             s_ident[:, 0:64], start=True, stop=True)
                                B.matmul(pbE[:, idx * CH:(idx + 1) * CH],
                                         s_kc[b:b + 64, kcol:kcol + CH],
                                         s_qc[b:b + 64, kcol:kcol + CH],
                                         start=True, stop=False)
                                i2 = B.matmul(pbE[:, idx * CH:(idx + 1) * CH],
                                              s_ks[b:b + 64, kcol:kcol + CH],
                                              s_qs[b:b + 64, kcol:kcol + CH],
                                              start=False, stop=True)
                        blk("pe", i2)
                        wait("vector", "pe")
                        i2 = B.tensor_tensor(s_em[:], pbE[:], s_mask[:], ALU.mult) if on("vector") else None
                        blk("dv", i2)
                        mark(egate)
                        wait_mark("tensor", ogate, "dv")
                        wait("tensor", "dv", max(marks[egate]["dv"], state["read_dv"]))
                        i2 = None
                        if on("tensor") and K_SUB >= 4:
                            for idx, hh in enumerate((0, 2, 1, 3)):
                                h = q4 * 4 + hh
                                b = (h % 2) * 64
                                pr = h // 2
                                ocol = ((h % 4) // 2) * 128
                                qcol = pr * GSZ + tok0
                                B.matmul(pbO[b:b + 64, ocol:ocol + CH],
                                         s_vtm[:, j * D + h * 64: j * D + h * 64 + 64],
                                         s_em[:, idx * CH:(idx + 1) * CH],
                                         start=True, stop=False)
                                B.matmul(pbO[b:b + 64, ocol:ocol + CH],
                                         s_scb[b:b + 64, pr * 64: pr * 64 + 64],
                                         s_qc[b:b + 64, qcol:qcol + CH],
                                         start=False, stop=False)
                                i2 = B.matmul(pbO[b:b + 64, ocol:ocol + CH],
                                              s_ssb[b:b + 64, pr * 64: pr * 64 + 64],
                                              s_qs[b:b + 64, qcol:qcol + CH],
                                              start=False, stop=True)
                        blk("pe", i2)
                        wait("vector", "pe")
                        i2 = None
                        if on("vector") and K_SUB >= 5:
                            gtok = g * GSZ + tok0
                            pr0 = q4 * 2
                            B.tensor_copy(s_oat[:, pr0 * T + gtok: pr0 * T + gtok + CH], pbO[:, 0:CH])
                            i2 = B.tensor_copy(s_oat[:, (pr0 + 1) * T + gtok: (pr0 + 1) * T + gtok + CH],
                                               pbO[:, CH:2 * CH])
                        blk("dv", i2)
                        mark(ogate)

                wait("tensor", "dv", max(marks.get("tr_free", cnt)["dv"],
                                         marks.get("uadd", cnt)["dv"],
                                         state["read_dv"]))
                i3 = None
                if on("tensor"):
                    for h in range(H):
                        b = (h % 2) * 64
                        pr = h // 2
                        ckc = (2 * pr) * 128 + b
                        cks = (2 * pr + 1) * 128 + b
                        B.matmul(p_uc[b:b + 64, pr * 64: pr * 64 + 64],
                                 s_ktm[:, ckc:ckc + 64],
                                 s_vtm[:, j * D + h * 64: j * D + h * 64 + 64],
                                 start=True, stop=True)
                        i3 = B.matmul(p_us[b:b + 64, pr * 64: pr * 64 + 64],
                                      s_ktm[:, cks:cks + 64],
                                      s_vtm[:, j * D + h * 64: j * D + h * 64 + 64],
                                      start=True, stop=True)
                blk("pe", i3)
                state["write_pe"] = cnt["pe"]
                wait("vector", "pe")
                i3 = None
                if on("vector"):
                    B.tensor_add(s_stc[:], s_stc[:], p_uc[:])
                    i3 = B.tensor_add(s_sts[:], s_sts[:], p_us[:])
                blk("dv", i3)
                mark("uadd")

            def do_group(g, own):
                gbase = g * GSZ
                grp_idx = (NG + g) if own else g
                if own:
                    ln_group(xT, gbase, s_xn, gbase, T)
                    xn_src, xn_col, xn_str = s_xn, gbase, T
                else:
                    ln_group(xTp, gbase, s_xnp, 0, GSZ)
                    xn_src, xn_col, xn_str = s_xnp, 0, GSZ
                wait("tensor", "dv")
                wait_mark("tensor", "wkv", "sw")
                lrpe_project(s_wk, xn_src, xn_col, xn_str, s_kc, s_ks, grp_idx)
                if own:
                    lrpe_project(s_wq, xn_src, xn_col, xn_str, s_qc, s_qs, grp_idx)
                v_project(xn_src, xn_col, xn_str)
                for j in range(CPG):
                    chunk_attn(g, j, own)

            if K_LEVEL >= 2:
                for g in range(NG):
                    do_group(g, own=False)
                    if K_LEVEL == 2:
                        break
            if K_LEVEL >= 4:
                for g in range(NG):
                    do_group(g, own=True)
                    if K_LEVEL == 4:
                        break

            if K_LEVEL < 6:
                dump = {"xn": s_xn, "oat": s_oat, "kc": s_kc, "ks": s_ks,
                        "qc": s_qc, "qs": s_qs, "stc": s_stc, "vtm": s_vtm,
                        "scb": s_scb, "cosu": s_cosu, "sinu": s_sinu,
                        "coscb": s_coscb, "sincb": s_sincb,
                        }.get(K_DUMP, s_xn)
                ncols = dump.shape[1]
                i = None
                if on("gpsimd"):
                    B.wait_ge(sems["dv"], cnt["dv"])
                    B.wait_ge(sems["ac"], cnt["ac"])
                    if ncols >= T:
                        i = B.dma_start(fm_x3(outT)[:, 0:ncols // T, :],
                                        sb3(dump[:, 0:(ncols // T) * T], T))
                    else:
                        i = B.dma_start(fm_x3(outT)[:, 0:1, 0:ncols], dump[:, 0:ncols])
                blk("so", i, 16)
                wait("gpsimd", "so")
                return
            wait("tensor", "dv")
            wait("gpsimd", "dv")
            wait("gpsimd", "pe")
            wcast(sb3(s_g1[:], HD), fm_w(g1_d))
            wcast(s_g2[:], g2_d.ap())
            mark("gw")
            for sub in range(NG):
                pb = p_b0 if (sub % 2 == 0) else p_b1
                gate = f"g1{sub % 2}"
                wait_mark("tensor", gate, "dv")
                wait_mark("tensor", "bc_free", "dv")
                wait_mark("tensor", "gw", "sw")
                i2 = None
                if on("tensor"):
                    for c in range(NCH):
                        i2 = B.matmul(pb[0:64, 0:GSZ],
                                      s_g1[:, c * HD:(c + 1) * HD],
                                      s_xn[:, c * T + sub * GSZ: c * T + (sub + 1) * GSZ],
                                      start=(c == 0), stop=(c == NCH - 1))
                blk("pe", i2)
                wait("vector", "pe")
                i2 = B.tensor_copy(s_xng1[:, sub * GSZ:(sub + 1) * GSZ], pb[0:64, 0:GSZ]) if on("vector") else None
                blk("dv", i2)
                mark(gate)
            for sub in range(NG):
                for m in range(NCH):
                    pb = p_b0 if (m % 2 == 0) else p_b1
                    gate = f"g2{m % 2}"
                    wait_mark("tensor", gate, "ac")
                    wait_mark("tensor", "g10", "dv")
                    wait_mark("tensor", "g11", "dv")
                    i2 = None
                    if on("tensor"):
                        i2 = B.matmul(pb[:, 0:GSZ], s_g2[:, m * 128:(m + 1) * 128],
                                      s_xng1[:, sub * GSZ:(sub + 1) * GSZ],
                                      start=True, stop=True)
                    blk("pe", i2)
                    wait("scalar", "pe")
                    i2 = None
                    if on("scalar"):
                        i2 = B.activation(s_gate[:, m * T + sub * GSZ: m * T + (sub + 1) * GSZ],
                                          pb[:, 0:GSZ], AF.Sigmoid)
                    blk("ac", i2)
                    mark(gate)

            if K_LEVEL < 7:
                i = None
                if on("gpsimd"):
                    B.wait_ge(sems["ac"], cnt["ac"])
                    B.wait_ge(sems["dv"], cnt["dv"])
                    i = B.dma_start(fm_w(outT)[:, 0:1, 0:T], sb3(s_xn[:, 0:T], T))
                blk("so", i, 16)
                wait("gpsimd", "so")
                return
            wait("gpsimd", "pe")
            wcast(sb3(s_wo[:], D), fm_w(wo_d))
            mark("wo_w")
            for sub in range(NG):
                for m in range(NCH):
                    pb = p_b2 if (m % 2 == 0) else p_b3
                    gate = f"wo{m % 2}"
                    wait_mark("tensor", gate, "dv")
                    wait_mark("tensor", "wo_w", "sw")
                    wait("tensor", "dv", marks.get("oe1", cnt)["dv"])
                    i2 = None
                    if on("tensor"):
                        for c in range(NCH):
                            i2 = B.matmul(pb[:, 0:GSZ],
                                          s_wo[:, c * D + m * 128: c * D + m * 128 + 128],
                                          s_oat[:, c * T + sub * GSZ: c * T + (sub + 1) * GSZ],
                                          start=(c == 0), stop=(c == NCH - 1))
                    blk("pe", i2)
                    wait("vector", "pe")
                    wait("vector", "ac")
                    i2 = None
                    if on("vector"):
                        i2 = B.tensor_tensor(s_spre[:, m * GSZ:(m + 1) * GSZ], pb[:, 0:GSZ],
                                             s_gate[:, m * T + sub * GSZ: m * T + (sub + 1) * GSZ], ALU.mult)
                    blk("dv", i2)
                    mark(gate)
                i2 = None
                if on("vector"):
                    for c in range(NCH):
                        i2 = B.tensor_copy(s_spb[:, c * GSZ:(c + 1) * GSZ], s_spre[:, c * GSZ:(c + 1) * GSZ])
                blk("dv", i2)
                wait("scalar", "dv")
                i2 = None
                if on("scalar"):
                    for c in range(NCH):
                        i2 = B.activation(s_sp2[:, c * GSZ:(c + 1) * GSZ], s_spb[:, c * GSZ:(c + 1) * GSZ], AF.Square)
                blk("ac", i2)
                wait("tensor", "dv")
                wait("tensor", "ac")
                wait_mark("tensor", "b7_free", "dv")
                i2 = None
                if on("tensor"):
                    for c in range(NCH):
                        B.matmul(p_b7[0:2, 0:GSZ], s_ones2[:], s_spb[:, c * GSZ:(c + 1) * GSZ],
                                 start=(c == 0), stop=(c == NCH - 1))
                    for c in range(NCH):
                        i2 = B.matmul(p_b0[0:2, 0:GSZ], s_ones2[:], s_sp2[:, c * GSZ:(c + 1) * GSZ],
                                      start=(c == 0), stop=(c == NCH - 1))
                blk("pe", i2)
                wait("vector", "pe")
                i2 = None
                if on("vector"):
                    B.tensor_scalar(s_rows[:, 0:GSZ], p_b7[0:2, 0:GSZ], 1.0 / D, None, ALU.mult)
                    B.tensor_scalar(s_rows[:, GSZ:2 * GSZ], p_b0[0:2, 0:GSZ], 1.0 / D, None, ALU.mult)
                    B.tensor_tensor(s_rows[:, 2 * GSZ:3 * GSZ], s_rows[:, 0:GSZ], s_rows[:, 0:GSZ], ALU.mult)
                    i2 = B.tensor_sub(s_rows[:, GSZ:2 * GSZ], s_rows[:, GSZ:2 * GSZ], s_rows[:, 2 * GSZ:3 * GSZ])
                blk("dv", i2)
                mark("b7_free")
                wait("scalar", "dv")
                i2 = None
                if on("scalar"):
                    B.activation(s_rows[:, 2 * GSZ:3 * GSZ], s_rows[:, GSZ:2 * GSZ], AF.Ln, bias=f32i(EPS))
                    i2 = B.activation(s_rows[:, GSZ:2 * GSZ], s_rows[:, 2 * GSZ:3 * GSZ], AF.Exp, scale=f32i(-0.5))
                blk("ac", i2)
                wait("vector", "ac")
                i2 = B.tensor_mul(s_rows[:, 2 * GSZ:3 * GSZ], s_rows[:, 0:GSZ], s_rows[:, GSZ:2 * GSZ]) if on("vector") else None
                blk("dv", i2)
                wait("tensor", "dv")
                i2 = None
                if on("tensor"):
                    B.matmul(p_b0[:, 0:GSZ], s_half2[:], s_rows[:, GSZ:2 * GSZ], start=True, stop=True)
                    i2 = B.matmul(p_b1[:, 0:GSZ], s_half2[:], s_rows[:, 2 * GSZ:3 * GSZ], start=True, stop=True)
                blk("pe", i2)
                wait("vector", "pe")
                i2 = None
                if on("vector"):
                    for c in range(NCH):
                        B.tensor_tensor(s_spre[:, c * GSZ:(c + 1) * GSZ], s_spre[:, c * GSZ:(c + 1) * GSZ],
                                        p_b0[:, 0:GSZ], ALU.mult)
                        B.tensor_tensor(s_spre[:, c * GSZ:(c + 1) * GSZ], s_spre[:, c * GSZ:(c + 1) * GSZ],
                                        p_b1[:, 0:GSZ], ALU.subtract)
                        B.tensor_scalar(s_spre[:, c * GSZ:(c + 1) * GSZ],
                                        s_spre[:, c * GSZ:(c + 1) * GSZ],
                                        s_gamma[:, c:c + 1], None, ALU.mult)
                        i2 = B.tensor_scalar(s_s[:, c * T + sub * GSZ: c * T + (sub + 1) * GSZ],
                                             s_spre[:, c * GSZ:(c + 1) * GSZ],
                                             s_beta[:, c:c + 1], None, ALU.add)
                blk("dv", i2)
                mark("bc_free")

            def wgu3(t, hc):
                return t.ap().rearrange("(c p) n -> p c n", p=128)[:, :, hc * 128:(hc + 1) * 128]

            wait("gpsimd", "pe")
            wait("gpsimd", "dv")
            for half in range(2):
                th0 = half * TH
                for hc in range(HC):
                    bufp = hc % 2
                    wcol = bufp * 2 * NCH * 128
                    wait_mark("gpsimd", f"wgu{bufp}", "pe")
                    inst = None
                    if on("gpsimd"):
                        inst = B.dma_start(
                            s_wgu[:, wcol: wcol + NCH * 128].rearrange("p (c n) -> p c n", n=128),
                            wgu3(wgT_d, hc))
                    blk("sw", inst, 16)
                    inst = None
                    if on("gpsimd"):
                        inst = B.dma_start(
                            s_wgu[:, wcol + NCH * 128: wcol + 2 * NCH * 128].rearrange("p (c n) -> p c n", n=128),
                            wgu3(wuT_d, hc))
                    blk("sw", inst, 16)
                    wait("tensor", "sw")
                    for ns in range(2):
                        item = hc * 2 + ns
                        ip = item % 2
                        pbg = p_b0 if ip == 0 else p_b1
                        pbu = p_b2 if ip == 0 else p_b3
                        wait_mark("tensor", f"ffne{ip}", "dv")
                        if hc == 0 and ns == 0:
                            wait_mark("tensor", "bc_free", "dv")
                            wait_mark("tensor", "fd0", "dv")
                            wait_mark("tensor", "fd1", "dv")
                        i2 = None
                        if on("tensor"):
                            for c in range(NCH):
                                st, sp = (c == 0), (c == NCH - 1)
                                xcol = c * T + th0 + ns * GSZ
                                B.matmul(pbg[:, 0:GSZ],
                                         s_wgu[:, wcol + c * 128: wcol + c * 128 + 128],
                                         s_xn[:, xcol:xcol + GSZ], start=st, stop=sp)
                                i2 = B.matmul(pbu[:, 0:GSZ],
                                              s_wgu[:, wcol + NCH * 128 + c * 128: wcol + NCH * 128 + c * 128 + 128],
                                              s_xn[:, xcol:xcol + GSZ], start=st, stop=sp)
                        blk("pe", i2)
                        if ns == 1:
                            mark(f"wgu{bufp}")
                        wait("scalar", "pe")
                        wait_mark("scalar", f"silf{ip}", "dv")
                        i2 = None
                        if on("scalar"):
                            i2 = B.activation(s_silu[:, ip * GSZ:(ip + 1) * GSZ], pbg[:, 0:GSZ],
                                              AF.Silu, scale=f32i(QS))
                        blk("ac", i2)
                        wait("vector", "ac")
                        i2 = None
                        if on("vector"):
                            sh = s_h1 if hc < HC // 2 else s_h2
                            hcl = hc % (HC // 2)
                            i2 = B.tensor_tensor(sh[:, hcl * TH + ns * GSZ: hcl * TH + (ns + 1) * GSZ],
                                                 s_silu[:, ip * GSZ:(ip + 1) * GSZ], pbu[:, 0:GSZ], ALU.mult)
                        blk("dv", i2)
                        mark(f"ffne{ip}")
                        mark(f"silf{ip}")
                        mark("ffnh")

                for m in range(NCH):
                    bufp = m % 2
                    wdcol = bufp * HC * 128
                    pba, pbb = (p_b0, p_b1) if bufp == 0 else (p_b2, p_b3)
                    wait_mark("gpsimd", f"wd{bufp}", "pe")
                    inst = None
                    if on("gpsimd"):
                        inst = B.dma_start(
                            s_wd[:, wdcol: wdcol + HC * 128].rearrange("p (c n) -> p c n", n=128),
                            wdT_d.ap().rearrange("(c p) n -> p c n", p=128)[:, :, m * 128:(m + 1) * 128])
                    blk("sw", inst, 16)
                    wait("tensor", "sw")
                    wait_mark("tensor", f"fd{bufp}", "dv")
                    wait_mark("tensor", "ffnh", "dv")
                    i2 = None
                    if on("tensor"):
                        for hk in range(HC):
                            st, sp = (hk == 0), (hk == HC - 1)
                            for ns in range(2):
                                pbx = pba if ns == 0 else pbb
                                sh = s_h1 if hk < HC // 2 else s_h2
                                hkl = hk % (HC // 2)
                                i2 = B.matmul(pbx[:, 0:GSZ],
                                              s_wd[:, wdcol + hk * 128: wdcol + hk * 128 + 128],
                                              sh[:, hkl * TH + ns * GSZ: hkl * TH + (ns + 1) * GSZ],
                                              start=st, stop=sp)
                    blk("pe", i2)
                    mark(f"wd{bufp}")
                    wait("scalar", "pe")
                    wait("scalar", "dv")
                    wait("scalar", "so")
                    i2 = None
                    if on("scalar"):
                        B.activation(s_fin[:, 0:GSZ], pba[:, 0:GSZ], AF.Silu, scale=f32i(QS * QS))
                        i2 = B.activation(s_fin[:, GSZ:2 * GSZ], pbb[:, 0:GSZ], AF.Silu, scale=f32i(QS * QS))
                    blk("ac", i2)
                    wait_mark("sync", "xres_free", "dv")
                    wait("sync", "pe")
                    inst = None
                    if on("sync"):
                        inst = B.dma_start(
                            sb3(s_xres[:, 0:2 * GSZ], 2 * GSZ),
                            fm_w(xT)[:, m:m + 1, th0:th0 + 2 * GSZ])
                    blk("sx", inst, 16)
                    wait("vector", "ac")
                    wait("vector", "sx")
                    i2 = None
                    if on("vector"):
                        for ns in range(2):
                            B.tensor_tensor(s_fin[:, ns * GSZ:(ns + 1) * GSZ],
                                            s_fin[:, ns * GSZ:(ns + 1) * GSZ],
                                            s_s[:, m * T + th0 + ns * GSZ: m * T + th0 + (ns + 1) * GSZ],
                                            ALU.mult)
                            i2 = B.tensor_add(s_fin[:, ns * GSZ:(ns + 1) * GSZ],
                                              s_fin[:, ns * GSZ:(ns + 1) * GSZ],
                                              s_xres[:, ns * GSZ:(ns + 1) * GSZ])
                    blk("dv", i2)
                    mark(f"fd{bufp}")
                    mark(f"fin{bufp}")
                    mark("xres_free")
                    wait("sync", "dv")
                    inst = None
                    if on("sync"):
                        inst = B.dma_start(
                            fm_w(outT)[:, m:m + 1, th0:th0 + 2 * GSZ],
                            sb3(s_fin[:, 0:2 * GSZ], 2 * GSZ))
                    blk("so", inst, 16)

            wait("sync", "so")

        @block.sync
        def _(b):
            program("sync", b)

        @block.gpsimd
        def _(b):
            program("gpsimd", b)

        @block.tensor
        def _(b):
            program("tensor", b)

        @block.vector
        def _(b):
            program("vector", b)

        @block.scalar
        def _(b):
            program("scalar", b)

    return nc


_NC_CACHE = []


def kernel(L0_attn_x, L0_attn_Wq, L0_attn_Wk, L0_attn_Wv, L0_attn_Wo,
           L0_attn_G1, L0_attn_G2, L0_attn_gamma, L0_attn_beta, L0_attn_theta,
           L0_ffn_w_gate, L0_ffn_w_up, L0_ffn_w_down):
    x = np.asarray(L0_attn_x, np.float32)
    Bb, N, _ = x.shape

    wq = np.ascontiguousarray(np.asarray(L0_attn_Wq, np.float32))
    wk = np.ascontiguousarray(np.asarray(L0_attn_Wk, np.float32))
    wv = np.ascontiguousarray(np.asarray(L0_attn_Wv, np.float32))
    wo = np.ascontiguousarray(np.asarray(L0_attn_Wo, np.float32))
    g1 = np.ascontiguousarray(np.asarray(L0_attn_G1, np.float32))
    g2 = np.ascontiguousarray(np.asarray(L0_attn_G2, np.float32))
    gamma = np.ascontiguousarray(np.asarray(L0_attn_gamma, np.float32).reshape(NCH, 128).T)
    beta = np.ascontiguousarray(np.asarray(L0_attn_beta, np.float32).reshape(NCH, 128).T)
    theta = np.ascontiguousarray(np.asarray(L0_attn_theta, np.float32).reshape(D).reshape(NCH, 128).T)
    wgT = np.ascontiguousarray(np.asarray(L0_ffn_w_gate, np.int32).T)
    wuT = np.ascontiguousarray(np.asarray(L0_ffn_w_up, np.int32).T)
    wdT = np.ascontiguousarray(np.asarray(L0_ffn_w_down, np.int32).T)

    threp = np.ascontiguousarray(np.repeat(theta, GSZ // NCH, axis=1))

    if not _NC_CACHE:
        _NC_CACHE.append(build_program())
    nc = _NC_CACHE[0]

    in_maps = []
    for core in range(8):
        b, halfi = core // 2, core % 2
        own = np.ascontiguousarray(x[b, halfi * T:(halfi + 1) * T, :].T)
        if halfi == 1:
            pre = np.ascontiguousarray(x[b, 0:T, :].T)
        else:
            pre = np.zeros((D, T), np.float32)
        base0 = halfi * T
        bases = np.array([g * GSZ for g in range(NG)] +
                         [base0 + g * GSZ for g in range(NG)], np.float32)
        cb = np.ascontiguousarray(np.broadcast_to(
            np.tile(np.repeat(bases, 8), NCH), (128, GSZ)))
        in_maps.append({
            "xT": own, "xTp": pre, "wq": wq, "wk": wk, "wv": wv, "wo": wo,
            "g1": g1, "g2": g2, "gamma": gamma, "beta": beta, "theta": theta,
            "cbrow": cb, "threp": threp, "wgT": wgT, "wuT": wuT, "wdT": wdT,
        })

    res = run_bass_kernel_spmd(nc, in_maps, core_ids=list(range(8)))
    out = np.empty((Bb, N, D), np.float32)
    for core in range(8):
        b, halfi = core // 2, core % 2
        out[b, halfi * T:(halfi + 1) * T, :] = np.asarray(res.results[core]["outT"]).T
    return out


# version 13 (final)
# speedup vs baseline: 1.0319x; 1.0319x over previous
import math

import numpy as np
import ml_dtypes
import concourse.bass as bass
import concourse.mybir as mybir
from concourse.bass_utils import run_bass_kernel_spmd

F32 = mybir.dt.float32
BF16 = mybir.dt.bfloat16
I32 = mybir.dt.int32
AF = mybir.ActivationFunctionType
ALU = mybir.AluOpType

D = 1024
H = 16
HD = 64
NCH = 8
T = 2048
GSZ = 512
NG = T // GSZ
CH = 128
CPG = GSZ // CH
FFN_H = 3072
HC = FFN_H // 128
TH = T // 2
EPS = 1e-5
QS = 1.0 / 127.0

PI = math.pi
MAGIC = 12582912.0
INV2PI = float(np.float32(1.0 / (2 * PI)))
NTWO_PI = float(np.float32(-(2 * PI)))
f32i = np.float32

ENGS = ["sync", "gpsimd", "tensor", "vector", "scalar"]
import os as _os
K_LEVEL = int(_os.environ.get("K_LEVEL", "7"))
K_SUB = int(_os.environ.get("K_SUB", "5"))
K_DUMP = _os.environ.get("K_DUMP", "")


def build_program(detect_races=False):
    nc = bass.Bass("TRN2", target_bir_lowering=False, detect_race_conditions=detect_races)

    xT = nc.dram_tensor("xT", [D, T], F32, kind="ExternalInput")
    xTp = nc.dram_tensor("xTp", [D, T], F32, kind="ExternalInput")
    wq_d = nc.dram_tensor("wq", [D, D], F32, kind="ExternalInput")
    wk_d = nc.dram_tensor("wk", [D, D], F32, kind="ExternalInput")
    wv_d = nc.dram_tensor("wv", [D, D], F32, kind="ExternalInput")
    wo_d = nc.dram_tensor("wo", [D, D], F32, kind="ExternalInput")
    g1_d = nc.dram_tensor("g1", [D, HD], F32, kind="ExternalInput")
    g2_d = nc.dram_tensor("g2", [HD, D], F32, kind="ExternalInput")
    gamma_d = nc.dram_tensor("gamma", [128, NCH], F32, kind="ExternalInput")
    beta_d = nc.dram_tensor("beta", [128, NCH], F32, kind="ExternalInput")
    theta_d = nc.dram_tensor("theta", [128, NCH], F32, kind="ExternalInput")
    cbrow_d = nc.dram_tensor("cbrow", [128, GSZ], F32, kind="ExternalInput")
    threp_d = nc.dram_tensor("threp", [128, GSZ], F32, kind="ExternalInput")
    wgT_d = nc.dram_tensor("wgT", [D, FFN_H], I32, kind="ExternalInput")
    wuT_d = nc.dram_tensor("wuT", [D, FFN_H], I32, kind="ExternalInput")
    wdT_d = nc.dram_tensor("wdT", [FFN_H, D], I32, kind="ExternalInput")
    outT = nc.dram_tensor("outT", [D, T], F32, kind="ExternalOutput")

    ident_d = nc.inline_tensor(np.eye(128, dtype=ml_dtypes.bfloat16), "c_ident")
    ones2_d = nc.inline_tensor(np.ones((128, 2), dtype=ml_dtypes.bfloat16), "c_ones2")
    half2_d = nc.inline_tensor(np.full((2, 128), 0.5, dtype=np.float32), "c_half2")
    tri = np.triu(np.ones((CH, CH), dtype=np.float32))
    mask_d = nc.inline_tensor(np.concatenate([tri] * 4, axis=1).astype(ml_dtypes.bfloat16), "c_mask")
    posu_d = nc.inline_tensor(
        np.broadcast_to(np.arange(GSZ, dtype=np.float32), (2, GSZ)).copy(), "c_posu")
    magic_d = nc.inline_tensor(np.full((128, 1), 12582912.0, dtype=np.float32), "c_magic")

    cur = [16416]

    def place(name, shape, dtype, at=None):
        nbytes = shape[1] * (2 if dtype in (BF16,) else 4)
        if at is None:
            at = cur[0]
            cur[0] += (nbytes + 31) // 32 * 32
            assert cur[0] <= 229344, f"sbuf overflow at {name}: {cur[0]}"
        return nc.alloc_sbuf_tensor_at(name, shape, dtype, offset=at)

    s_ident = place("s_ident", [128, 128], BF16)
    s_ones2 = place("s_ones2", [128, 2], BF16)
    s_half2 = place("s_half2", [2, 128], F32)
    s_mask = place("s_mask", [128, 4 * CH], BF16)
    s_gamma = place("s_gamma", [128, NCH], F32)
    s_beta = place("s_beta", [128, NCH], F32)
    s_theta = place("s_theta", [128, NCH], F32)
    s_thsc = place("s_thsc", [128, NCH], F32)
    s_magic = place("s_magic", [128, 1], F32)
    s_coscb = place("s_coscb", [128, GSZ], F32)
    s_sincb = place("s_sincb", [128, GSZ], F32)
    tab0 = cur[0]
    s_cosu = place("s_cosu", [128, NCH * GSZ], BF16)
    s_sinu = place("s_sinu", [128, NCH * GSZ], BF16)
    misc0 = cur[0]
    s_tsc1 = place("s_tsc1", [128, GSZ], F32)
    s_tsc2 = place("s_tsc2", [128, GSZ], F32)
    s_tsc3 = place("s_tsc3", [128, GSZ], F32)
    s_tsc4 = place("s_tsc4", [128, GSZ], F32)
    posu2_at = cur[0]
    s_posu2 = place("s_posu2", [2, GSZ], F32)
    s_rows = place("s_rows", [2, 3 * GSZ], F32)
    xg0 = cur[0]
    s_xg = place("s_xg", [128, NCH * GSZ], F32)
    s_xn = place("s_xn", [128, NCH * T], BF16)
    oat0 = cur[0]
    s_oat = place("s_oat", [128, NCH * T], BF16)
    qk0 = cur[0]
    s_qc = place("s_qc", [128, NCH * GSZ], BF16)
    s_qs = place("s_qs", [128, NCH * GSZ], BF16)
    s_kc = place("s_kc", [128, NCH * GSZ], BF16)
    s_ks = place("s_ks", [128, NCH * GSZ], BF16)
    w0 = cur[0]
    s_wq = place("s_wq", [128, NCH * D], BF16)
    s_wk = place("s_wk", [128, NCH * D], BF16)
    s_wv = place("s_wv", [128, NCH * D], BF16)
    small0 = cur[0]
    s_ktm = place("s_ktm", [128, 2 * D], BF16)
    s_qraw = place("s_qraw", [128, 2 * GSZ], BF16, at=posu2_at)
    s_uc = place("s_uc", [128, GSZ], BF16)
    s_us = place("s_us", [128, GSZ], BF16)
    s_em = place("s_em", [128, 4 * CH], BF16)
    s_scb = place("s_scb", [128, 512], BF16)
    s_ssb = place("s_ssb", [128, 512], BF16)

    top_used = cur[0]
    assert top_used <= 229344, f"over budget: {top_used}"

    s_stc = place("s_stc", [128, 512], F32, at=misc0 + 2 * GSZ * 4)
    s_sts = place("s_sts", [128, 512], F32, at=misc0 + 3 * GSZ * 4)
    s_xgb = place("s_xgb", [128, NCH * GSZ], BF16, at=qk0 + NCH * GSZ * 2)
    s_xg2 = place("s_xg2", [128, NCH * GSZ], BF16, at=qk0)
    s_xnp = place("s_xnp", [128, NCH * GSZ], BF16, at=oat0)
    s_vtm = place("s_vtm", [128, CPG * D], BF16, at=xg0)
    s_wo = place("s_wo", [128, NCH * D], BF16, at=w0)
    s_s = place("s_s", [128, NCH * T], BF16, at=w0 + NCH * D * 2)
    s_gate = place("s_gate", [128, NCH * T], BF16, at=qk0)
    s_spre = place("s_spre", [128, NCH * GSZ], F32, at=xg0)
    s_spb = place("s_spb", [128, NCH * GSZ], BF16, at=tab0)
    s_sp2 = place("s_sp2", [128, NCH * GSZ], BF16, at=tab0 + NCH * GSZ * 2)
    s_xng1 = place("s_xng1", [64, T], BF16, at=misc0)
    s_g1 = place("s_g1", [128, NCH * HD], BF16, at=misc0 + 2 * GSZ * 4)
    s_g2 = place("s_g2", [64, D], BF16, at=misc0 + 3 * GSZ * 4)
    s_h1 = place("s_h1", [128, (HC // 2) * TH], BF16, at=qk0)
    s_h2 = place("s_h2", [128, (HC // 2) * TH], BF16, at=oat0)
    s_wgu = place("s_wgu", [128, 4 * NCH * 128], BF16, at=tab0)
    s_wd = place("s_wd", [128, 2 * HC * 128], BF16, at=xg0)
    s_silu = place("s_silu", [128, 2 * GSZ], F32, at=misc0)
    s_fin = place("s_fin", [128, 2 * GSZ], F32, at=misc0 + 2 * GSZ * 4)
    s_xres = place("s_xres", [128, 2 * GSZ], F32, at=small0)

    P = nc.alloc_psum_tensor
    p_b0 = P("p_b0", [128, 512], F32)
    p_b1 = P("p_b1", [128, 512], F32)
    p_b2 = P("p_b2", [128, 512], F32)
    p_b3 = P("p_b3", [128, 512], F32)
    p_uc = P("p_uc", [128, 512], F32)
    p_us = P("p_us", [128, 512], F32)
    p_tr = P("p_tr", [128, 1024], BF16)
    p_b7 = P("p_b7", [128, 512], F32)

    with (
        nc.semaphore("sx") as sem_sx,
        nc.semaphore("sw") as sem_sw,
        nc.semaphore("pe") as sem_pe,
        nc.semaphore("dv") as sem_dv,
        nc.semaphore("ac") as sem_ac,
        nc.semaphore("so") as sem_so,
        nc.Block() as block,
    ):
        sems = {"sx": sem_sx, "sw": sem_sw, "pe": sem_pe,
                "dv": sem_dv, "ac": sem_ac, "so": sem_so}

        def program(eng, B):
            cnt = {k: 0 for k in sems}
            marks = {}
            state = {"started": False, "read_dv": 0, "read_ac": 0, "write_pe": 0}

            def on(e):
                return eng == e

            def wait(e, sem, val=None):
                v = cnt[sem] if val is None else val
                if eng == e and v > 0:
                    B.wait_ge(sems[sem], v)

            def blk(sem, last_inst, n=1):
                cnt[sem] += n
                if last_inst is not None:
                    last_inst.then_inc(sems[sem], n)
                return cnt[sem]

            def mark(name):
                marks[name] = dict(cnt)

            def wait_mark(e, name, sem):
                m = marks.get(name)
                if m is not None:
                    wait(e, sem, m[sem])

            def fm_w(t):
                return t.ap().rearrange("(c p) n -> p c n", p=128)

            def sb3(ap_slice, n):
                return ap_slice.rearrange("p (c n) -> p c n", n=n)

            def fm_x3(t):
                return t.ap().rearrange("(c p) n -> p c n", p=128)

            def ld(dst, src):
                inst = B.dma_start(dst, src) if on("sync") else None
                blk("sx", inst, 16)

            ld(s_ident[:], ident_d.ap())
            ld(s_ones2[:], ones2_d.ap())
            ld(s_half2[:], half2_d.ap())
            ld(s_mask[:], mask_d.ap())
            ld(s_gamma[:], gamma_d.ap())
            ld(s_beta[:], beta_d.ap())
            ld(s_theta[:], theta_d.ap())
            ld(s_posu2[:], posu_d.ap())
            ld(s_magic[:], magic_d.ap())
            mark("p0")

            def wcast(dst, src):
                inst = B.dma_start(dst, src) if on("gpsimd") else None
                blk("sw", inst, 16)

            wcast(sb3(s_wk[:], D), fm_w(wk_d))
            wcast(sb3(s_wv[:], D), fm_w(wv_d))
            wcast(sb3(s_wq[:], D), fm_w(wq_d))
            mark("wkv")

            wait_mark("vector", "p0", "sx")
            i = B.tensor_scalar(s_thsc[:], s_theta[:], INV2PI, None, ALU.mult) if on("vector") else None
            blk("dv", i)

            NGB = 2 * NG
            inst = B.dma_start(s_tsc4[:], cbrow_d.ap()) if on("sync") else None
            blk("sx", inst, 16)
            inst = B.dma_start(s_tsc3[:], threp_d.ap()) if on("sync") else None
            blk("sx", inst, 16)
            wait("vector", "sx")
            i = None
            if on("vector"):
                B.tensor_tensor(s_tsc1[:], s_tsc4[:], s_tsc3[:], ALU.mult)
                B.tensor_scalar(s_tsc2[:], s_tsc1[:], INV2PI, None, ALU.mult)
                B.tensor_scalar(s_tsc2[:], s_tsc2[:], s_magic[:, 0:1], None, ALU.add)
                B.tensor_scalar(s_tsc2[:], s_tsc2[:], -MAGIC, None, ALU.add)
                B.scalar_tensor_tensor(s_sincb[:], s_tsc2[:], NTWO_PI, s_tsc1[:], ALU.mult, ALU.add)
                B.tensor_scalar(s_tsc3[:], s_tsc1[:], INV2PI, None, ALU.mult)
                B.tensor_scalar(s_tsc3[:], s_tsc3[:], 0.25, None, ALU.add)
                B.tensor_scalar(s_tsc3[:], s_tsc3[:], s_magic[:, 0:1], None, ALU.add)
                B.tensor_scalar(s_tsc3[:], s_tsc3[:], -MAGIC, None, ALU.add)
                B.scalar_tensor_tensor(s_coscb[:], s_tsc3[:], NTWO_PI, s_tsc1[:], ALU.mult, ALU.add)
                B.tensor_scalar(s_sincb[:], s_sincb[:], PI, None, ALU.min)
                B.tensor_scalar(s_sincb[:], s_sincb[:], -PI, None, ALU.max)
                B.tensor_scalar(s_coscb[:], s_coscb[:], PI / 2, None, ALU.min)
                i = B.tensor_scalar(s_coscb[:], s_coscb[:], -1.5 * PI, None, ALU.max)
            blk("dv", i)
            wait("scalar", "dv")
            i = B.activation(s_sincb[:], s_sincb[:], AF.Sin) if on("scalar") else None
            blk("ac", i)
            i = B.activation(s_coscb[:], s_coscb[:], AF.Sin, bias=f32i(PI / 2)) if on("scalar") else None
            blk("ac", i)

            wait("tensor", "dv")
            i = B.matmul(p_b7[:, 0:GSZ], s_half2[:], s_posu2[:], start=True, stop=True) if on("tensor") else None
            blk("pe", i)
            wait("vector", "pe")
            i = B.tensor_copy(s_tsc4[:], p_b7[:, 0:GSZ]) if on("vector") else None
            blk("dv", i)
            for c in range(NCH):
                col = c * GSZ
                wait("vector", "ac")
                i = None
                if on("vector"):
                    B.tensor_scalar(s_tsc1[:], s_tsc4[:], s_theta[:, c:c + 1], None, ALU.mult)
                    B.tensor_scalar(s_tsc2[:], s_tsc4[:], s_thsc[:, c:c + 1], None, ALU.mult)
                    B.tensor_scalar(s_tsc2[:], s_tsc2[:], s_magic[:, 0:1], None, ALU.add)
                    B.tensor_scalar(s_tsc2[:], s_tsc2[:], -MAGIC, None, ALU.add)
                    B.scalar_tensor_tensor(s_tsc2[:], s_tsc2[:], NTWO_PI, s_tsc1[:], ALU.mult, ALU.add)
                    B.tensor_scalar(s_tsc3[:], s_tsc4[:], s_thsc[:, c:c + 1], None, ALU.mult)
                    B.tensor_scalar(s_tsc3[:], s_tsc3[:], 0.25, None, ALU.add)
                    B.tensor_scalar(s_tsc3[:], s_tsc3[:], s_magic[:, 0:1], None, ALU.add)
                    B.tensor_scalar(s_tsc3[:], s_tsc3[:], -MAGIC, None, ALU.add)
                    B.scalar_tensor_tensor(s_tsc1[:], s_tsc3[:], NTWO_PI, s_tsc1[:], ALU.mult, ALU.add)
                    B.tensor_scalar(s_tsc2[:], s_tsc2[:], PI, None, ALU.min)
                    B.tensor_scalar(s_tsc2[:], s_tsc2[:], -PI, None, ALU.max)
                    B.tensor_scalar(s_tsc1[:], s_tsc1[:], PI / 2, None, ALU.min)
                    i = B.tensor_scalar(s_tsc1[:], s_tsc1[:], -1.5 * PI, None, ALU.max)
                blk("dv", i)
                wait("scalar", "dv")
                i = B.activation(s_sinu[:, col:col + GSZ], s_tsc2[:], AF.Sin) if on("scalar") else None
                blk("ac", i)
                i = B.activation(s_cosu[:, col:col + GSZ], s_tsc1[:], AF.Sin, bias=f32i(PI / 2)) if on("scalar") else None
                blk("ac", i)
            i = None
            if on("vector"):
                B.memset(s_stc[:], 0.0)
                i = B.memset(s_sts[:], 0.0)
            blk("dv", i)
            mark("uadd")
            mark("tables")

            def ln_group(xsrc, gcol, xn_dst, xn_col, xn_str):
                wait_mark("sync", "xg_free", "dv")
                wait("sync", "pe")
                inst = None
                if on("sync"):
                    inst = B.dma_start(
                        s_xg[:].rearrange("p (c n) -> p c n", n=GSZ),
                        fm_x3(xsrc)[:, :, gcol:gcol + GSZ])
                blk("sx", inst, 16)
                wait("gpsimd", "pe")
                inst = None
                if on("gpsimd"):
                    inst = B.dma_start(
                        s_xgb[:].rearrange("p (c n) -> p c n", n=GSZ),
                        fm_x3(xsrc)[:, :, gcol:gcol + GSZ])
                blk("sw", inst, 16)
                wait("scalar", "sx")
                wait("scalar", "pe")
                i2 = None
                if on("scalar"):
                    for c in range(NCH):
                        i2 = B.activation(s_xg2[:, c * GSZ:(c + 1) * GSZ], s_xg[:, c * GSZ:(c + 1) * GSZ], AF.Square)
                blk("ac", i2)
                wait("tensor", "sx")
                wait("tensor", "sw")
                wait("tensor", "ac")
                wait("tensor", "dv")
                wait_mark("tensor", "b7_free", "dv")
                wait_mark("tensor", "bc_free", "dv")
                i2 = None
                if on("tensor"):
                    for c in range(NCH):
                        B.matmul(p_b7[0:2, 0:GSZ], s_ones2[:], s_xgb[:, c * GSZ:(c + 1) * GSZ],
                                 start=(c == 0), stop=(c == NCH - 1))
                    for c in range(NCH):
                        i2 = B.matmul(p_b0[0:2, 0:GSZ], s_ones2[:], s_xg2[:, c * GSZ:(c + 1) * GSZ],
                                      start=(c == 0), stop=(c == NCH - 1))
                blk("pe", i2)
                mark("stats_done")
                wait("vector", "pe")
                i2 = None
                if on("vector"):
                    B.tensor_scalar(s_rows[:, 0:GSZ], p_b7[0:2, 0:GSZ], 1.0 / D, None, ALU.mult)
                    B.tensor_scalar(s_rows[:, GSZ:2 * GSZ], p_b0[0:2, 0:GSZ], 1.0 / D, None, ALU.mult)
                    B.tensor_tensor(s_rows[:, 2 * GSZ:3 * GSZ], s_rows[:, 0:GSZ], s_rows[:, 0:GSZ], ALU.mult)
                    i2 = B.tensor_sub(s_rows[:, GSZ:2 * GSZ], s_rows[:, GSZ:2 * GSZ], s_rows[:, 2 * GSZ:3 * GSZ])
                blk("dv", i2)
                mark("b7_free")
                wait("scalar", "dv")
                i2 = None
                if on("scalar"):
                    B.activation(s_rows[:, 2 * GSZ:3 * GSZ], s_rows[:, GSZ:2 * GSZ], AF.Ln, bias=f32i(EPS))
                    i2 = B.activation(s_rows[:, GSZ:2 * GSZ], s_rows[:, 2 * GSZ:3 * GSZ], AF.Exp, scale=f32i(-0.5))
                blk("ac", i2)
                wait("vector", "ac")
                i2 = B.tensor_mul(s_rows[:, 2 * GSZ:3 * GSZ], s_rows[:, 0:GSZ], s_rows[:, GSZ:2 * GSZ]) if on("vector") else None
                blk("dv", i2)
                wait("tensor", "dv")
                i2 = None
                if on("tensor"):
                    B.matmul(p_b0[:, 0:GSZ], s_half2[:], s_rows[:, GSZ:2 * GSZ], start=True, stop=True)
                    i2 = B.matmul(p_b1[:, 0:GSZ], s_half2[:], s_rows[:, 2 * GSZ:3 * GSZ], start=True, stop=True)
                blk("pe", i2)
                wait("vector", "pe")
                i2 = None
                if on("vector"):
                    for c in range(NCH):
                        B.tensor_tensor(s_xg[:, c * GSZ:(c + 1) * GSZ], s_xg[:, c * GSZ:(c + 1) * GSZ],
                                        p_b0[:, 0:GSZ], ALU.mult)
                        i2 = B.tensor_tensor(xn_dst[:, xn_col + c * xn_str: xn_col + c * xn_str + GSZ],
                                             s_xg[:, c * GSZ:(c + 1) * GSZ], p_b1[:, 0:GSZ], ALU.subtract)
                blk("dv", i2)
                mark("xg_free")
                mark("bc_free")

            def lrpe_project(w_sb, xn_src, xn_col, xn_str, dst_c, dst_s, grp_idx):
                for m in range(NCH):
                    pb = (p_b0, p_b1, p_b2, p_b3)[m % 4]
                    gate = f"pj{m % 4}"
                    wait_mark("tensor", gate, "dv")
                    i2 = None
                    if on("tensor"):
                        for c in range(NCH):
                            i2 = B.matmul(pb[:, 0:GSZ],
                                          w_sb[:, c * D + m * 128: c * D + m * 128 + 128],
                                          xn_src[:, xn_col + c * xn_str: xn_col + c * xn_str + GSZ],
                                          start=(c == 0), stop=(c == NCH - 1))
                    blk("pe", i2)
                    qr = (m % 2) * GSZ
                    wait("scalar", "pe")
                    wait_mark("scalar", f"qraw{m % 2}", "dv")
                    i2 = B.activation(s_qraw[:, qr:qr + GSZ], pb[:, 0:GSZ], AF.Silu) if on("scalar") else None
                    blk("ac", i2)
                    wait("vector", "ac")
                    i2 = None
                    if on("vector"):
                        cbcol = m * 64 + grp_idx * 8
                        B.tensor_mul(s_uc[:], s_qraw[:, qr:qr + GSZ], s_cosu[:, m * GSZ:(m + 1) * GSZ])
                        B.tensor_mul(s_us[:], s_qraw[:, qr:qr + GSZ], s_sinu[:, m * GSZ:(m + 1) * GSZ])
                        B.tensor_scalar(s_tsc1[:], s_uc[:], s_coscb[:, cbcol:cbcol + 1], None, ALU.mult)
                        B.tensor_scalar(s_tsc2[:], s_us[:], s_sincb[:, cbcol:cbcol + 1], None, ALU.mult)
                        B.tensor_sub(dst_c[:, m * GSZ:(m + 1) * GSZ], s_tsc1[:], s_tsc2[:])
                        B.tensor_scalar(s_tsc1[:], s_us[:], s_coscb[:, cbcol:cbcol + 1], None, ALU.mult)
                        i2 = B.scalar_tensor_tensor(dst_s[:, m * GSZ:(m + 1) * GSZ], s_uc[:],
                                                    s_sincb[:, cbcol:cbcol + 1], s_tsc1[:], ALU.mult, ALU.add)
                    blk("dv", i2)
                    mark(gate)
                    mark(f"qraw{m % 2}")

            def v_project(xn_src, xn_col, xn_str):
                for j in range(CPG):
                    for half in range(2):
                        it = j * 2 + half
                        pb = (p_b0, p_b1, p_b2, p_b3)[it % 4]
                        gate = f"pj{it % 4}"
                        wait_mark("tensor", gate, "dv")
                        i2 = None
                        if on("tensor"):
                            for c in range(NCH):
                                i2 = B.matmul(
                                    pb[:, 0:GSZ],
                                    xn_src[:, xn_col + c * xn_str + j * CH: xn_col + c * xn_str + j * CH + CH],
                                    s_wv[:, c * D + half * GSZ: c * D + (half + 1) * GSZ],
                                    start=(c == 0), stop=(c == NCH - 1))
                        blk("pe", i2)
                        wait("vector", "pe")
                        i2 = None
                        if on("vector"):
                            i2 = B.tensor_copy(s_vtm[:, j * D + half * GSZ: j * D + (half + 1) * GSZ], pb[:, 0:GSZ])
                        blk("dv", i2)
                        mark(gate)

            def chunk_attn(g, j, own):
                tok0 = j * CH

                for half in range(2):
                    wait_mark("tensor", "tr_free", "ac")
                    i2 = None
                    if on("tensor"):
                        for t in range(4):
                            c = half * 4 + t
                            B.transpose(p_tr[:, t * 256:t * 256 + 128],
                                        s_kc[:, c * GSZ + tok0: c * GSZ + tok0 + CH], s_ident[:])
                            i2 = B.transpose(p_tr[:, t * 256 + 128:t * 256 + 256],
                                             s_ks[:, c * GSZ + tok0: c * GSZ + tok0 + CH], s_ident[:])
                    blk("pe", i2)
                    wait("scalar", "pe")
                    i2 = None
                    if on("scalar"):
                        i2 = B.copy(s_ktm[:, half * 1024:(half + 1) * 1024], p_tr[:])
                    blk("ac", i2)
                    mark("tr_free")

                if own and K_SUB >= 2:
                    wait("scalar", "dv", marks.get("uadd", cnt)["dv"])
                    wait("scalar", "pe", state["write_pe"])
                    i2 = None
                    if on("scalar"):
                        B.copy(s_scb[:], s_stc[:])
                        i2 = B.copy(s_ssb[:], s_sts[:])
                    blk("ac", i2)
                    state["read_ac"] = cnt["ac"]

                    for q4 in range(4 if K_SUB >= 3 else 0):
                        pbE = p_b0 if (q4 % 2 == 0) else p_b1
                        pbO = p_b2 if (q4 % 2 == 0) else p_b3
                        egate = f"ee{q4 % 2}"
                        ogate = f"oe{q4 % 2}"
                        wait_mark("tensor", egate, "dv")
                        wait_mark("tensor", "bc_free", "dv")
                        for _pj in range(4):
                            wait_mark("tensor", f"pj{_pj}", "dv")
                        i2 = None
                        if on("tensor"):
                            for idx, hh in enumerate((0, 2, 1, 3)):
                                h = q4 * 4 + hh
                                c, b = h // 2, (h % 2) * 64
                                kcol = c * GSZ + tok0
                                if (idx == 0 and q4 > 0) or idx == 2:
                                    B.matmul(p_b7[0:64, 0:64], s_ident[:, 0:64],
                                             s_ident[:, 0:64], start=True, stop=True)
                                B.matmul(pbE[:, idx * CH:(idx + 1) * CH],
                                         s_kc[b:b + 64, kcol:kcol + CH],
                                         s_qc[b:b + 64, kcol:kcol + CH],
                                         start=True, stop=False)
                                i2 = B.matmul(pbE[:, idx * CH:(idx + 1) * CH],
                                              s_ks[b:b + 64, kcol:kcol + CH],
                                              s_qs[b:b + 64, kcol:kcol + CH],
                                              start=False, stop=True)
                        blk("pe", i2)
                        wait("vector", "pe")
                        i2 = B.tensor_tensor(s_em[:], pbE[:], s_mask[:], ALU.mult) if on("vector") else None
                        blk("dv", i2)
                        mark(egate)
                        wait_mark("tensor", ogate, "ac")
                        wait("tensor", "dv", marks[egate]["dv"])
                        wait("tensor", "ac", state.get("read_ac", 0))
                        i2 = None
                        if on("tensor") and K_SUB >= 4:
                            for idx, hh in enumerate((0, 2, 1, 3)):
                                h = q4 * 4 + hh
                                b = (h % 2) * 64
                                pr = h // 2
                                ocol = ((h % 4) // 2) * 128
                                qcol = pr * GSZ + tok0
                                B.matmul(pbO[b:b + 64, ocol:ocol + CH],
                                         s_vtm[:, j * D + h * 64: j * D + h * 64 + 64],
                                         s_em[:, idx * CH:(idx + 1) * CH],
                                         start=True, stop=False)
                                B.matmul(pbO[b:b + 64, ocol:ocol + CH],
                                         s_scb[b:b + 64, pr * 64: pr * 64 + 64],
                                         s_qc[b:b + 64, qcol:qcol + CH],
                                         start=False, stop=False)
                                i2 = B.matmul(pbO[b:b + 64, ocol:ocol + CH],
                                              s_ssb[b:b + 64, pr * 64: pr * 64 + 64],
                                              s_qs[b:b + 64, qcol:qcol + CH],
                                              start=False, stop=True)
                        blk("pe", i2)
                        wait("scalar", "pe")
                        i2 = None
                        if on("scalar") and K_SUB >= 5:
                            gtok = g * GSZ + tok0
                            pr0 = q4 * 2
                            B.copy(s_oat[:, pr0 * T + gtok: pr0 * T + gtok + CH], pbO[:, 0:CH])
                            i2 = B.copy(s_oat[:, (pr0 + 1) * T + gtok: (pr0 + 1) * T + gtok + CH],
                                        pbO[:, CH:2 * CH])
                        blk("ac", i2)
                        mark(ogate)

                wait("tensor", "ac", max(marks.get("tr_free", cnt)["ac"], state.get("read_ac", 0)))
                wait("tensor", "dv", marks.get("uadd", cnt)["dv"])
                i3 = None
                if on("tensor"):
                    for h in range(H):
                        b = (h % 2) * 64
                        pr = h // 2
                        ckc = (2 * pr) * 128 + b
                        cks = (2 * pr + 1) * 128 + b
                        B.matmul(p_uc[b:b + 64, pr * 64: pr * 64 + 64],
                                 s_ktm[:, ckc:ckc + 64],
                                 s_vtm[:, j * D + h * 64: j * D + h * 64 + 64],
                                 start=True, stop=True)
                        i3 = B.matmul(p_us[b:b + 64, pr * 64: pr * 64 + 64],
                                      s_ktm[:, cks:cks + 64],
                                      s_vtm[:, j * D + h * 64: j * D + h * 64 + 64],
                                      start=True, stop=True)
                blk("pe", i3)
                state["write_pe"] = cnt["pe"]
                wait("vector", "pe")
                i3 = None
                if on("vector"):
                    B.tensor_add(s_stc[:], s_stc[:], p_uc[:])
                    i3 = B.tensor_add(s_sts[:], s_sts[:], p_us[:])
                blk("dv", i3)
                mark("uadd")

            def do_group(g, own):
                gbase = g * GSZ
                grp_idx = (NG + g) if own else g
                if own:
                    ln_group(xT, gbase, s_xn, gbase, T)
                    xn_src, xn_col, xn_str = s_xn, gbase, T
                else:
                    ln_group(xTp, gbase, s_xnp, 0, GSZ)
                    xn_src, xn_col, xn_str = s_xnp, 0, GSZ
                wait("tensor", "dv")
                wait_mark("tensor", "wkv", "sw")
                lrpe_project(s_wk, xn_src, xn_col, xn_str, s_kc, s_ks, grp_idx)
                if own:
                    lrpe_project(s_wq, xn_src, xn_col, xn_str, s_qc, s_qs, grp_idx)
                v_project(xn_src, xn_col, xn_str)
                for j in range(CPG):
                    chunk_attn(g, j, own)

            if K_LEVEL >= 2:
                for g in range(NG):
                    do_group(g, own=False)
                    if K_LEVEL == 2:
                        break
            if K_LEVEL >= 4:
                for g in range(NG):
                    do_group(g, own=True)
                    if K_LEVEL == 4:
                        break

            if K_LEVEL < 6:
                dump = {"xn": s_xn, "oat": s_oat, "kc": s_kc, "ks": s_ks,
                        "qc": s_qc, "qs": s_qs, "stc": s_stc, "vtm": s_vtm,
                        "scb": s_scb, "cosu": s_cosu, "sinu": s_sinu,
                        "coscb": s_coscb, "sincb": s_sincb,
                        }.get(K_DUMP, s_xn)
                ncols = dump.shape[1]
                i = None
                if on("gpsimd"):
                    B.wait_ge(sems["dv"], cnt["dv"])
                    B.wait_ge(sems["ac"], cnt["ac"])
                    if ncols >= T:
                        i = B.dma_start(fm_x3(outT)[:, 0:ncols // T, :],
                                        sb3(dump[:, 0:(ncols // T) * T], T))
                    else:
                        i = B.dma_start(fm_x3(outT)[:, 0:1, 0:ncols], dump[:, 0:ncols])
                blk("so", i, 16)
                wait("gpsimd", "so")
                return
            wait("tensor", "dv")
            wait("gpsimd", "dv")
            wait("gpsimd", "pe")
            wcast(sb3(s_g1[:], HD), fm_w(g1_d))
            wcast(s_g2[:], g2_d.ap())
            mark("gw")
            for sub in range(NG):
                pb = p_b0 if (sub % 2 == 0) else p_b1
                gate = f"g1{sub % 2}"
                wait_mark("tensor", gate, "dv")
                wait_mark("tensor", "bc_free", "dv")
                wait_mark("tensor", "gw", "sw")
                i2 = None
                if on("tensor"):
                    for c in range(NCH):
                        i2 = B.matmul(pb[0:64, 0:GSZ],
                                      s_g1[:, c * HD:(c + 1) * HD],
                                      s_xn[:, c * T + sub * GSZ: c * T + (sub + 1) * GSZ],
                                      start=(c == 0), stop=(c == NCH - 1))
                blk("pe", i2)
                wait("vector", "pe")
                i2 = B.tensor_copy(s_xng1[:, sub * GSZ:(sub + 1) * GSZ], pb[0:64, 0:GSZ]) if on("vector") else None
                blk("dv", i2)
                mark(gate)
            for sub in range(NG):
                for m in range(NCH):
                    pb = p_b0 if (m % 2 == 0) else p_b1
                    gate = f"g2{m % 2}"
                    wait_mark("tensor", gate, "ac")
                    wait_mark("tensor", "g10", "dv")
                    wait_mark("tensor", "g11", "dv")
                    i2 = None
                    if on("tensor"):
                        i2 = B.matmul(pb[:, 0:GSZ], s_g2[:, m * 128:(m + 1) * 128],
                                      s_xng1[:, sub * GSZ:(sub + 1) * GSZ],
                                      start=True, stop=True)
                    blk("pe", i2)
                    wait("scalar", "pe")
                    i2 = None
                    if on("scalar"):
                        i2 = B.activation(s_gate[:, m * T + sub * GSZ: m * T + (sub + 1) * GSZ],
                                          pb[:, 0:GSZ], AF.Sigmoid)
                    blk("ac", i2)
                    mark(gate)

            if K_LEVEL < 7:
                i = None
                if on("gpsimd"):
                    B.wait_ge(sems["ac"], cnt["ac"])
                    B.wait_ge(sems["dv"], cnt["dv"])
                    i = B.dma_start(fm_w(outT)[:, 0:1, 0:T], sb3(s_xn[:, 0:T], T))
                blk("so", i, 16)
                wait("gpsimd", "so")
                return
            wait("gpsimd", "pe")
            wcast(sb3(s_wo[:], D), fm_w(wo_d))
            mark("wo_w")
            for sub in range(NG):
                for m in range(NCH):
                    pb = p_b2 if (m % 2 == 0) else p_b3
                    gate = f"wo{m % 2}"
                    wait_mark("tensor", gate, "dv")
                    wait_mark("tensor", "wo_w", "sw")
                    wait("tensor", "ac", marks.get("oe1", cnt)["ac"])
                    i2 = None
                    if on("tensor"):
                        for c in range(NCH):
                            i2 = B.matmul(pb[:, 0:GSZ],
                                          s_wo[:, c * D + m * 128: c * D + m * 128 + 128],
                                          s_oat[:, c * T + sub * GSZ: c * T + (sub + 1) * GSZ],
                                          start=(c == 0), stop=(c == NCH - 1))
                    blk("pe", i2)
                    wait("vector", "pe")
                    wait("vector", "ac")
                    i2 = None
                    if on("vector"):
                        i2 = B.tensor_tensor(s_spre[:, m * GSZ:(m + 1) * GSZ], pb[:, 0:GSZ],
                                             s_gate[:, m * T + sub * GSZ: m * T + (sub + 1) * GSZ], ALU.mult)
                    blk("dv", i2)
                    mark(gate)
                i2 = None
                if on("vector"):
                    for c in range(NCH):
                        i2 = B.tensor_copy(s_spb[:, c * GSZ:(c + 1) * GSZ], s_spre[:, c * GSZ:(c + 1) * GSZ])
                blk("dv", i2)
                wait("scalar", "dv")
                i2 = None
                if on("scalar"):
                    for c in range(NCH):
                        i2 = B.activation(s_sp2[:, c * GSZ:(c + 1) * GSZ], s_spb[:, c * GSZ:(c + 1) * GSZ], AF.Square)
                blk("ac", i2)
                wait("tensor", "dv")
                wait("tensor", "ac")
                wait_mark("tensor", "b7_free", "dv")
                i2 = None
                if on("tensor"):
                    for c in range(NCH):
                        B.matmul(p_b7[0:2, 0:GSZ], s_ones2[:], s_spb[:, c * GSZ:(c + 1) * GSZ],
                                 start=(c == 0), stop=(c == NCH - 1))
                    for c in range(NCH):
                        i2 = B.matmul(p_b0[0:2, 0:GSZ], s_ones2[:], s_sp2[:, c * GSZ:(c + 1) * GSZ],
                                      start=(c == 0), stop=(c == NCH - 1))
                blk("pe", i2)
                wait("vector", "pe")
                i2 = None
                if on("vector"):
                    B.tensor_scalar(s_rows[:, 0:GSZ], p_b7[0:2, 0:GSZ], 1.0 / D, None, ALU.mult)
                    B.tensor_scalar(s_rows[:, GSZ:2 * GSZ], p_b0[0:2, 0:GSZ], 1.0 / D, None, ALU.mult)
                    B.tensor_tensor(s_rows[:, 2 * GSZ:3 * GSZ], s_rows[:, 0:GSZ], s_rows[:, 0:GSZ], ALU.mult)
                    i2 = B.tensor_sub(s_rows[:, GSZ:2 * GSZ], s_rows[:, GSZ:2 * GSZ], s_rows[:, 2 * GSZ:3 * GSZ])
                blk("dv", i2)
                mark("b7_free")
                wait("scalar", "dv")
                i2 = None
                if on("scalar"):
                    B.activation(s_rows[:, 2 * GSZ:3 * GSZ], s_rows[:, GSZ:2 * GSZ], AF.Ln, bias=f32i(EPS))
                    i2 = B.activation(s_rows[:, GSZ:2 * GSZ], s_rows[:, 2 * GSZ:3 * GSZ], AF.Exp, scale=f32i(-0.5))
                blk("ac", i2)
                wait("vector", "ac")
                i2 = B.tensor_mul(s_rows[:, 2 * GSZ:3 * GSZ], s_rows[:, 0:GSZ], s_rows[:, GSZ:2 * GSZ]) if on("vector") else None
                blk("dv", i2)
                wait("tensor", "dv")
                i2 = None
                if on("tensor"):
                    B.matmul(p_b0[:, 0:GSZ], s_half2[:], s_rows[:, GSZ:2 * GSZ], start=True, stop=True)
                    i2 = B.matmul(p_b1[:, 0:GSZ], s_half2[:], s_rows[:, 2 * GSZ:3 * GSZ], start=True, stop=True)
                blk("pe", i2)
                wait("vector", "pe")
                i2 = None
                if on("vector"):
                    for c in range(NCH):
                        B.tensor_tensor(s_spre[:, c * GSZ:(c + 1) * GSZ], s_spre[:, c * GSZ:(c + 1) * GSZ],
                                        p_b0[:, 0:GSZ], ALU.mult)
                        B.tensor_tensor(s_spre[:, c * GSZ:(c + 1) * GSZ], s_spre[:, c * GSZ:(c + 1) * GSZ],
                                        p_b1[:, 0:GSZ], ALU.subtract)
                        B.tensor_scalar(s_spre[:, c * GSZ:(c + 1) * GSZ],
                                        s_spre[:, c * GSZ:(c + 1) * GSZ],
                                        s_gamma[:, c:c + 1], None, ALU.mult)
                        i2 = B.tensor_scalar(s_s[:, c * T + sub * GSZ: c * T + (sub + 1) * GSZ],
                                             s_spre[:, c * GSZ:(c + 1) * GSZ],
                                             s_beta[:, c:c + 1], None, ALU.add)
                blk("dv", i2)
                mark("bc_free")

            def wgu3(t, hc):
                return t.ap().rearrange("(c p) n -> p c n", p=128)[:, :, hc * 128:(hc + 1) * 128]

            wait("gpsimd", "pe")
            wait("gpsimd", "dv")
            for half in range(2):
                th0 = half * TH
                for hc in range(HC):
                    bufp = hc % 2
                    wcol = bufp * 2 * NCH * 128
                    wait_mark("gpsimd", f"wgu{bufp}", "pe")
                    inst = None
                    if on("gpsimd"):
                        inst = B.dma_start(
                            s_wgu[:, wcol: wcol + NCH * 128].rearrange("p (c n) -> p c n", n=128),
                            wgu3(wgT_d, hc))
                    blk("sw", inst, 16)
                    inst = None
                    if on("gpsimd"):
                        inst = B.dma_start(
                            s_wgu[:, wcol + NCH * 128: wcol + 2 * NCH * 128].rearrange("p (c n) -> p c n", n=128),
                            wgu3(wuT_d, hc))
                    blk("sw", inst, 16)
                    wait("tensor", "sw")
                    for ns in range(2):
                        item = hc * 2 + ns
                        ip = item % 2
                        pbg = p_b0 if ip == 0 else p_b1
                        pbu = p_b2 if ip == 0 else p_b3
                        wait_mark("tensor", f"ffne{ip}", "dv")
                        if hc == 0 and ns == 0:
                            wait_mark("tensor", "bc_free", "dv")
                            wait_mark("tensor", "fd0", "dv")
                            wait_mark("tensor", "fd1", "dv")
                        i2 = None
                        if on("tensor"):
                            for c in range(NCH):
                                st, sp = (c == 0), (c == NCH - 1)
                                xcol = c * T + th0 + ns * GSZ
                                B.matmul(pbg[:, 0:GSZ],
                                         s_wgu[:, wcol + c * 128: wcol + c * 128 + 128],
                                         s_xn[:, xcol:xcol + GSZ], start=st, stop=sp)
                                i2 = B.matmul(pbu[:, 0:GSZ],
                                              s_wgu[:, wcol + NCH * 128 + c * 128: wcol + NCH * 128 + c * 128 + 128],
                                              s_xn[:, xcol:xcol + GSZ], start=st, stop=sp)
                        blk("pe", i2)
                        if ns == 1:
                            mark(f"wgu{bufp}")
                        wait("scalar", "pe")
                        wait_mark("scalar", f"silf{ip}", "dv")
                        i2 = None
                        if on("scalar"):
                            i2 = B.activation(s_silu[:, ip * GSZ:(ip + 1) * GSZ], pbg[:, 0:GSZ],
                                              AF.Silu, scale=f32i(QS))
                        blk("ac", i2)
                        wait("vector", "ac")
                        i2 = None
                        if on("vector"):
                            sh = s_h1 if hc < HC // 2 else s_h2
                            hcl = hc % (HC // 2)
                            i2 = B.tensor_tensor(sh[:, hcl * TH + ns * GSZ: hcl * TH + (ns + 1) * GSZ],
                                                 s_silu[:, ip * GSZ:(ip + 1) * GSZ], pbu[:, 0:GSZ], ALU.mult)
                        blk("dv", i2)
                        mark(f"ffne{ip}")
                        mark(f"silf{ip}")
                        mark("ffnh")

                for m in range(NCH):
                    bufp = m % 2
                    wdcol = bufp * HC * 128
                    pba, pbb = (p_b0, p_b1) if bufp == 0 else (p_b2, p_b3)
                    wait_mark("gpsimd", f"wd{bufp}", "pe")
                    inst = None
                    if on("gpsimd"):
                        inst = B.dma_start(
                            s_wd[:, wdcol: wdcol + HC * 128].rearrange("p (c n) -> p c n", n=128),
                            wdT_d.ap().rearrange("(c p) n -> p c n", p=128)[:, :, m * 128:(m + 1) * 128])
                    blk("sw", inst, 16)
                    wait("tensor", "sw")
                    wait_mark("tensor", f"fd{bufp}", "dv")
                    wait_mark("tensor", "ffnh", "dv")
                    i2 = None
                    if on("tensor"):
                        for hk in range(HC):
                            st, sp = (hk == 0), (hk == HC - 1)
                            for ns in range(2):
                                pbx = pba if ns == 0 else pbb
                                sh = s_h1 if hk < HC // 2 else s_h2
                                hkl = hk % (HC // 2)
                                i2 = B.matmul(pbx[:, 0:GSZ],
                                              s_wd[:, wdcol + hk * 128: wdcol + hk * 128 + 128],
                                              sh[:, hkl * TH + ns * GSZ: hkl * TH + (ns + 1) * GSZ],
                                              start=st, stop=sp)
                    blk("pe", i2)
                    mark(f"wd{bufp}")
                    wait("scalar", "pe")
                    wait("scalar", "dv")
                    wait("scalar", "so")
                    i2 = None
                    if on("scalar"):
                        B.activation(s_fin[:, 0:GSZ], pba[:, 0:GSZ], AF.Silu, scale=f32i(QS * QS))
                        i2 = B.activation(s_fin[:, GSZ:2 * GSZ], pbb[:, 0:GSZ], AF.Silu, scale=f32i(QS * QS))
                    blk("ac", i2)
                    wait_mark("sync", "xres_free", "dv")
                    wait("sync", "pe")
                    inst = None
                    if on("sync"):
                        inst = B.dma_start(
                            sb3(s_xres[:, 0:2 * GSZ], 2 * GSZ),
                            fm_w(xT)[:, m:m + 1, th0:th0 + 2 * GSZ])
                    blk("sx", inst, 16)
                    wait("vector", "ac")
                    wait("vector", "sx")
                    i2 = None
                    if on("vector"):
                        for ns in range(2):
                            B.tensor_tensor(s_fin[:, ns * GSZ:(ns + 1) * GSZ],
                                            s_fin[:, ns * GSZ:(ns + 1) * GSZ],
                                            s_s[:, m * T + th0 + ns * GSZ: m * T + th0 + (ns + 1) * GSZ],
                                            ALU.mult)
                            i2 = B.tensor_add(s_fin[:, ns * GSZ:(ns + 1) * GSZ],
                                              s_fin[:, ns * GSZ:(ns + 1) * GSZ],
                                              s_xres[:, ns * GSZ:(ns + 1) * GSZ])
                    blk("dv", i2)
                    mark(f"fd{bufp}")
                    mark(f"fin{bufp}")
                    mark("xres_free")
                    wait("sync", "dv")
                    inst = None
                    if on("sync"):
                        inst = B.dma_start(
                            fm_w(outT)[:, m:m + 1, th0:th0 + 2 * GSZ],
                            sb3(s_fin[:, 0:2 * GSZ], 2 * GSZ))
                    blk("so", inst, 16)

            wait("sync", "so")

        @block.sync
        def _(b):
            program("sync", b)

        @block.gpsimd
        def _(b):
            program("gpsimd", b)

        @block.tensor
        def _(b):
            program("tensor", b)

        @block.vector
        def _(b):
            program("vector", b)

        @block.scalar
        def _(b):
            program("scalar", b)

    return nc


_NC_CACHE = []


def kernel(L0_attn_x, L0_attn_Wq, L0_attn_Wk, L0_attn_Wv, L0_attn_Wo,
           L0_attn_G1, L0_attn_G2, L0_attn_gamma, L0_attn_beta, L0_attn_theta,
           L0_ffn_w_gate, L0_ffn_w_up, L0_ffn_w_down):
    x = np.asarray(L0_attn_x, np.float32)
    Bb, N, _ = x.shape

    wq = np.ascontiguousarray(np.asarray(L0_attn_Wq, np.float32))
    wk = np.ascontiguousarray(np.asarray(L0_attn_Wk, np.float32))
    wv = np.ascontiguousarray(np.asarray(L0_attn_Wv, np.float32))
    wo = np.ascontiguousarray(np.asarray(L0_attn_Wo, np.float32))
    g1 = np.ascontiguousarray(np.asarray(L0_attn_G1, np.float32))
    g2 = np.ascontiguousarray(np.asarray(L0_attn_G2, np.float32))
    gamma = np.ascontiguousarray(np.asarray(L0_attn_gamma, np.float32).reshape(NCH, 128).T)
    beta = np.ascontiguousarray(np.asarray(L0_attn_beta, np.float32).reshape(NCH, 128).T)
    theta = np.ascontiguousarray(np.asarray(L0_attn_theta, np.float32).reshape(D).reshape(NCH, 128).T)
    wgT = np.ascontiguousarray(np.asarray(L0_ffn_w_gate, np.int32).T)
    wuT = np.ascontiguousarray(np.asarray(L0_ffn_w_up, np.int32).T)
    wdT = np.ascontiguousarray(np.asarray(L0_ffn_w_down, np.int32).T)

    threp = np.ascontiguousarray(np.repeat(theta, GSZ // NCH, axis=1))

    if not _NC_CACHE:
        _NC_CACHE.append(build_program())
    nc = _NC_CACHE[0]

    in_maps = []
    for core in range(8):
        b, halfi = core // 2, core % 2
        own = np.ascontiguousarray(x[b, halfi * T:(halfi + 1) * T, :].T)
        if halfi == 1:
            pre = np.ascontiguousarray(x[b, 0:T, :].T)
        else:
            pre = np.zeros((D, T), np.float32)
        base0 = halfi * T
        bases = np.array([g * GSZ for g in range(NG)] +
                         [base0 + g * GSZ for g in range(NG)], np.float32)
        cb = np.ascontiguousarray(np.broadcast_to(
            np.tile(np.repeat(bases, 8), NCH), (128, GSZ)))
        in_maps.append({
            "xT": own, "xTp": pre, "wq": wq, "wk": wk, "wv": wv, "wo": wo,
            "g1": g1, "g2": g2, "gamma": gamma, "beta": beta, "theta": theta,
            "cbrow": cb, "threp": threp, "wgT": wgT, "wuT": wuT, "wdT": wdT,
        })

    res = run_bass_kernel_spmd(nc, in_maps, core_ids=list(range(8)))
    out = np.empty((Bb, N, D), np.float32)
    for core in range(8):
        b, halfi = core // 2, core % 2
        out[b, halfi * T:(halfi + 1) * T, :] = np.asarray(res.results[core]["outT"]).T
    return out
